# version 11
# baseline (speedup 1.0000x reference)
import os
if "AXON_NCC_FLAGS" in os.environ:
    os.environ["AXON_NCC_FLAGS"] = os.environ["AXON_NCC_FLAGS"].replace(
        "--enable-ldw-opt=false", "--enable-ldw-opt=true")

import numpy as np
import ml_dtypes
from contextlib import ExitStack

import concourse.bass as bass
import concourse.mybir as mybir
from concourse.bass_utils import run_bass_kernel_spmd

F32 = mybir.dt.float32
BF16 = mybir.dt.bfloat16
I32 = mybir.dt.int32
AF = mybir.ActivationFunctionType
ALU = mybir.AluOpType
AX = mybir.AxisListType

BF = ml_dtypes.bfloat16

B, T, C = 4, 4096, 1024
TOK = 2048
NCH = 16
HEADS, DH = 16, 64
FFN_H = 3072
QS = 1.0 / 127.0
LN_EPS = 1e-5
ATT_EPS = 1e-6
L2_EPS = 1e-12
EXPC = 16.0

_PROG = {}
_LAST_RESULTS = None


def _ap_bcast(a, times):
    dims = [list(d) for d in a.ap]
    dims = dims[:-1] + [[0, times]] + [dims[-1]]
    return bass.AP(a.tensor, a.offset, dims)


def _ap_bcast_last(a, times):
    dims = [list(d) for d in a.ap] + [[0, times]]
    return bass.AP(a.tensor, a.offset, dims)


class H:

    def __init__(self, nc, st):
        self.nc = nc
        self.st = st
        self.prefix = ""
        self.dsem = st.enter_context(nc.semaphore("dsem"))
        self.gsem = st.enter_context(nc.semaphore("gsem"))
        self.ccsem = st.enter_context(nc.semaphore("ccsem"))
        self.dcnt = 0
        self.gcnt = 0
        self.nccs = 0
        self._sems = {}

    def sbt(self, name, shape, dt):
        return self.nc.sbuf_tensor(self.prefix + name, shape, dt)

    def pst(self, name, shape, dt=F32):
        return self.nc.psum_tensor(self.prefix + name, shape, dt)

    def sem(self, name):
        key = self.prefix + name
        if key not in self._sems:
            self._sems[key] = self.st.enter_context(self.nc.semaphore(key))
        return self._sems[key]

    def collective(self, kind, op, ins, outs):
        self.nc.gpsimd.collective_compute(
            kind, op, replica_groups=[[0, 1], [2, 3], [4, 5], [6, 7]],
            ins=ins, outs=outs).then_inc(self.ccsem)
        self.nccs += 1
        self.nc.gpsimd.wait_ge(self.ccsem, self.nccs)
        self.bar()

    def bar(self):
        self.nc.all_engine_barrier()

    def dma(self, out, in_):
        self.nc.sync.dma_start(out=out, in_=in_).then_inc(self.dsem, 16)
        self.dcnt += 16

    def gdma(self, out, in_):
        self.nc.gpsimd.dma_start(out=out, in_=in_).then_inc(self.gsem, 16)
        self.gcnt += 16

    def dwait(self):
        if self.dcnt:
            self.nc.sync.wait_ge(self.dsem, self.dcnt)
        if self.gcnt:
            self.nc.gpsimd.wait_ge(self.gsem, self.gcnt)

    def flush(self):
        self.dwait()
        self.bar()

    def mk_ident(self):
        nc, st = self.nc, self.st
        self.ident = st.enter_context(nc.sbuf_tensor("ident", [128, 128], BF16))
        self.ident2 = st.enter_context(nc.sbuf_tensor("ident2", [128, 128], BF16))
        self.ones_row = st.enter_context(nc.sbuf_tensor("ones_row", [1, 128], BF16))
        self.ones_col = st.enter_context(nc.sbuf_tensor("ones_col", [128, 1], BF16))
        with nc.sbuf_tensor("iota_a", [128, 128], I32) as ia, \
                nc.sbuf_tensor("iota_b", [128, 128], I32) as ib:
            nc.gpsimd.iota(ia.ap(), [[1, 128]], channel_multiplier=0)
            nc.gpsimd.iota(ib.ap(), [[0, 128]], channel_multiplier=1)
            nc.vector.memset(self.ones_row.ap(), 1.0)
            nc.vector.memset(self.ones_col.ap(), 1.0)
            self.bar()
            nc.vector.tensor_tensor(self.ident.ap(), ia.ap(), ib.ap(), ALU.is_equal)
            self.bar()
            nc.vector.tensor_scalar_mul(self.ident2.ap(), self.ident.ap(), 2.0)
            self.bar()

    def mk_consts(self, spec):
        nc = self.nc
        n = len(spec)
        self.ctile = self.st.enter_context(nc.sbuf_tensor("consts", [128, n], F32))
        self.cmap = {}
        for i, (k, v) in enumerate(spec.items()):
            nc.vector.memset(self.ctile.ap()[:, i:i + 1], float(v))
            self.cmap[k] = i
        self.bar()

    def cc(self, name):
        i = self.cmap[name]
        return self.ctile.ap()[:, i:i + 1]


    def transpose_tile(self, ps_bf, src_ap, dst_ap):
        nc = self.nc
        nc.tensor.transpose(ps_bf.ap(), src_ap, self.ident.ap())
        self.bar()
        nc.scalar.copy(dst_ap, ps_bf.ap())
        self.bar()

    def transpose_group(self, ps_g, srcs, dst_ap):
        nc = self.nc
        n = len(srcs)
        for j, s in enumerate(srcs):
            nc.tensor.transpose(ps_g.ap()[:, j, :], s, self.ident.ap())
        self.bar()
        nc.scalar.copy(dst_ap, ps_g.ap()[:, 0:n, :])
        self.bar()

    def vedge(self, ins):
        s = self.sem("vchain")
        key = self.prefix + "vchain_cnt"
        n = self._sems.get(key, 0) + 1
        self._sems[key] = n
        ins.then_inc(s, 1)
        self.nc.vector.wait_ge(s, n)

    def pipe(self, n, mms, stages, nbuf, name="d"):
        nc = self.nc
        eng = {"tensor": nc.tensor, "scalar": nc.scalar, "vector": nc.vector,
               "gpsimd": nc.gpsimd}
        ns = len(stages)
        sems = [self.sem(f"pp{name}_s{k}") for k in range(ns + 1)]
        key = self.prefix + "pipe_" + name
        base = self._sems.get(key, 0)
        self._sems[key] = base + n
        for i in range(n + ns):
            for k, (ename, fn) in enumerate(stages):
                u = i - 1 - k
                if 0 <= u < n:
                    e = eng[ename]
                    e.wait_ge(sems[k], base + u + 1)
                    ins = fn(u)
                    if ins is None:
                        ins = e.nop()
                    ins.then_inc(sems[k + 1], 1)
            if i < n:
                if i >= nbuf:
                    nc.tensor.wait_ge(sems[ns], base + i - nbuf + 1)
                ins = mms(i)
                ins.then_inc(sems[0], 1)
        self.bar()

    def pipe_transpose(self, groups, ps_a, ps_b):
        nc = self.nc
        bufs = [ps_a, ps_b]
        n = len(groups)

        def mms(i):
            srcs = groups[i][0]
            for j, s in enumerate(srcs):
                ins = nc.tensor.transpose(bufs[i % 2].ap()[:, j, :], s,
                                          self.ident.ap())
            return ins

        def cp(u):
            srcs_p, dst_p = groups[u]
            return nc.scalar.copy(dst_p, bufs[u % 2].ap()[:, 0:len(srcs_p), :])

        self.pipe(n, mms, [("scalar", cp)], 2, name="tr")

    def ln_batch_stats(self, s1t, s2t, sc1t, sc2t, n, cw, eps_cc):
        nc = self.nc
        nc.vector.tensor_scalar_mul(sc1t[:, 0:n], s1t[:, 0:n], 1.0 / cw)
        self.bar()
        nc.vector.tensor_mul(s1t[:, 0:n], sc1t[:, 0:n], sc1t[:, 0:n])
        self.bar()
        nc.vector.scalar_tensor_tensor(s2t[:, 0:n], s2t[:, 0:n], 1.0 / cw,
                                       s1t[:, 0:n], ALU.mult, ALU.subtract)
        self.bar()
        nc.scalar.activation(s2t[:, 0:n], s2t[:, 0:n], AF.Sqrt, bias=eps_cc)
        self.bar()
        nc.vector.reciprocal(sc2t[:, 0:n], s2t[:, 0:n])
        self.bar()

    def ln_psum(self, ps_ap, out_ap, s1, s2, junk_a, junk_b, eps_cc, sc1, sc2):
        nc = self.nc
        cw = ps_ap.shape[-1]
        nc.vector.tensor_scalar(junk_a, ps_ap, 1.0, 0.0, ALU.mult, ALU.add,
                                accum_out=s1)
        self.bar()
        nc.scalar.activation(junk_b, ps_ap, AF.Square, accum_out=s2)
        self.bar()
        nc.vector.tensor_scalar_mul(sc1, s1, 1.0 / cw)
        self.bar()
        nc.vector.tensor_mul(s1, sc1, sc1)
        self.bar()
        nc.vector.scalar_tensor_tensor(s2, s2, 1.0 / cw, s1, ALU.mult, ALU.subtract)
        self.bar()
        nc.scalar.activation(s2, s2, AF.Sqrt, bias=eps_cc)
        self.bar()
        nc.vector.reciprocal(sc2, s2)
        self.bar()
        nc.vector.tensor_scalar(out_ap, ps_ap, sc1, sc2, ALU.subtract, ALU.mult)
        self.bar()


def _nc():
    return bass.Bass("TRN2", target_bir_lowering=False, debug=False)


ALL_CONSTS = {"neg1": -1.0, "ln_eps": LN_EPS, "qs": QS, "s8": 0.125,
              "negC": -EXPC, "s32": 1.0 / 32.0}


def build_A():
    nc = _nc()
    xg = nc.dram_tensor("xg", [TOK, C], F32, kind="ExternalInput").ap()
    wqkvT = nc.dram_tensor("wqkvT", [C, 3 * C], F32, kind="ExternalInput").ap()
    o_qf = nc.dram_tensor("qf", [C, TOK], BF16, kind="ExternalOutput").ap()
    o_ktm = nc.dram_tensor("ktm", [TOK, C], BF16, kind="ExternalOutput").ap()
    o_vtm = nc.dram_tensor("vtm", [TOK, C], BF16, kind="ExternalOutput").ap()
    o_kvp = nc.dram_tensor("kvp", [64, HEADS * 65], F32, kind="ExternalOutput").ap()
    with ExitStack() as gst:
        h = H(nc, gst)
        h.mk_ident()
        h.mk_consts(ALL_CONSTS)
        emit_A(nc, h, xg, wqkvT, o_qf, o_ktm, o_vtm, o_kvp)
    return nc


def emit_A(nc, h, xg, wqkvT, o_qf, o_ktm, o_vtm, o_kvp, kvp_4d=False):
    with ExitStack() as st:
        w_sb = st.enter_context(h.sbt("w_sb", [128, 8, 3 * C], BF16))
        x_fm = st.enter_context(h.sbt("x_fm", [128, 8, TOK], BF16))
        t1 = [st.enter_context(h.sbt(f"t1_{j}", [128, 512], F32)) for j in range(3)]
        t2 = [st.enter_context(h.sbt(f"t2_{j}", [128, 512], F32)) for j in range(3)]

        with h.sbt("x_tm", [128, NCH, C], BF16) as x_tm, \
                h.pst("ps_ta", [128, 8, 128], BF16) as ps_ta, \
                h.pst("ps_tb", [128, 8, 128], BF16) as ps_tb:
            h.gdma(x_tm.ap(), xg.rearrange("(cc p) f -> p cc f", p=128))
            h.gdma(w_sb.ap(), wqkvT.rearrange("(dc p) e -> p dc e", p=128))
            h.flush()
            h.pipe_transpose(
                [([x_tm.ap()[:, cc, dc * 128:(dc + 1) * 128] for dc in range(8)],
                  x_fm.ap()[:, :, cc * 128:(cc + 1) * 128]) for cc in range(NCH)],
                ps_ta, ps_tb)

        qf = st.enter_context(h.sbt("qf_sb", [128, 8, TOK], BF16))
        ktm = st.enter_context(h.sbt("ktm_sb", [128, NCH, C], BF16))
        vex = st.enter_context(h.sbt("vex_sb", [128, NCH, HEADS * 65], BF16))
        nc.vector.memset(vex.ap().rearrange("p cc (hd f) -> p cc hd f", f=65)
                         [:, :, :, 64:65], 1.0)

        units = []
        for qc in range(8):
            for ts in range(4):
                units.append(("q",
                              [w_sb.ap()[:, dc, qc * 128:(qc + 1) * 128] for dc in range(8)],
                              [x_fm.ap()[:, dc, ts * 512:(ts + 1) * 512] for dc in range(8)],
                              qf.ap()[:, qc, ts * 512:(ts + 1) * 512]))
        for cc in range(NCH):
            for half in range(2):
                units.append(("k",
                              [x_fm.ap()[:, dc, cc * 128:(cc + 1) * 128] for dc in range(8)],
                              [w_sb.ap()[:, dc, C + half * 512:C + (half + 1) * 512] for dc in range(8)],
                              ktm.ap()[:, cc, half * 512:(half + 1) * 512]))
            for half in range(2):
                units.append(("v",
                              [x_fm.ap()[:, dc, cc * 128:(cc + 1) * 128] for dc in range(8)],
                              [w_sb.ap()[:, dc, 2 * C + half * 512:2 * C + (half + 1) * 512] for dc in range(8)],
                              vex.ap().rearrange("p cc2 (hd f) -> p cc2 hd f", f=65)
                              [:, cc, half * 8:(half + 1) * 8, 0:64]))

        with h.pst("ps_m0", [128, 512], F32) as ps_m0, \
                h.pst("ps_m1", [128, 512], F32) as ps_m1, \
                h.pst("ps_m2", [128, 512], F32) as ps_m2:
            bufs = [ps_m0, ps_m1, ps_m2]

            def mms(i):
                _, ls, rs, _ = units[i]
                for dc in range(8):
                    ins = nc.tensor.matmul(bufs[i % 3].ap(), ls[dc], rs[dc],
                                           start=(dc == 0), stop=(dc == 7))
                return ins

            def st1(u):
                kind, _, _, dst = units[u]
                pb = bufs[u % 3]
                if kind == "v":
                    return nc.scalar.copy(dst, pb.ap().rearrange("p (hd f) -> p hd f", f=64))
                nc.scalar.activation(t1[u % 3].ap(), pb.ap(), AF.Relu,
                                     scale=h.cc("neg1"))
                return nc.scalar.activation(t2[u % 3].ap(), t1[u % 3].ap(),
                                            AF.Exp, scale=h.cc("neg1"))

            def st2(u):
                kind, _, _, dst = units[u]
                if kind == "v":
                    return None
                return nc.vector.scalar_tensor_tensor(dst, bufs[u % 3].ap(), 0.0,
                                                      t2[u % 3].ap(),
                                                      ALU.max, ALU.add)

            h.pipe(len(units), mms, [("scalar", st1), ("vector", st2)], 3, name="qkv")

        with h.pst("ps_kv", [64, HEADS, 128], F32) as ps_kv, \
                h.sbt("kvp_sb", [64, HEADS * 65], F32) as kvp_sb:
            for hd in range(HEADS):
                for cc in range(NCH):
                    nc.tensor.matmul(ps_kv.ap()[:, hd, 0:65],
                                     ktm.ap()[:, cc, hd * 64:(hd + 1) * 64],
                                     vex.ap().rearrange("p cc2 (h2 f) -> p cc2 h2 f", f=65)
                                     [:, cc, hd, :],
                                     start=(cc == 0), stop=(cc == NCH - 1))
            h.bar()
            nc.vector.tensor_copy(kvp_sb.ap().rearrange("p (hd f) -> p hd f", f=65),
                                  ps_kv.ap()[:, :, 0:65])
            h.bar()
            if kvp_4d:
                sv = kvp_sb.ap()
                src_kv = bass.AP(sv.tensor, sv.offset,
                                 [[HEADS * 65, 64], [130, 8], [65, 2], [1, 65]])
                dst_kv = bass.AP(o_kvp.tensor, o_kvp.offset,
                                 [[520, 64], [65, 8], [520 * 64, 2], [1, 65]])
                h.dma(dst_kv, src_kv)
            else:
                h.dma(o_kvp, kvp_sb.ap())

        h.dma(o_qf.rearrange("(qc p) t -> p qc t", p=128), qf.ap())
        if o_ktm is not None:
            h.dma(o_ktm.rearrange("(cc p) f -> p cc f", p=128), ktm.ap())
        if o_vtm is not None:
            for cc in range(NCH):
                h.dma(o_vtm.rearrange("(cc p) f -> p cc f", p=128)[:, cc, :]
                      .rearrange("p (hd f) -> p hd f", f=64),
                      vex.ap().rearrange("p cc2 (hd f) -> p cc2 hd f", f=65)
                      [:, cc, :, 0:64])
        h.flush()


def build_B():
    nc = _nc()
    qf_d = nc.dram_tensor("qf", [C, TOK], BF16, kind="ExternalInput").ap()
    kvp0 = nc.dram_tensor("kvp0", [128, 8 * 65], F32, kind="ExternalInput").ap()
    kvp1 = nc.dram_tensor("kvp1", [128, 8 * 65], F32, kind="ExternalInput").ap()
    wprojT = nc.dram_tensor("wprojT", [C, C], F32, kind="ExternalInput").ap()
    bproj = nc.dram_tensor("bproj", [C], F32, kind="ExternalInput").ap()
    xg = nc.dram_tensor("xg", [TOK, C], F32, kind="ExternalInput").ap()
    o_x1 = nc.dram_tensor("x1", [TOK, C], BF16, kind="ExternalOutput").ap()
    with ExitStack() as gst:
        h = H(nc, gst)
        h.mk_ident()
        h.mk_consts(ALL_CONSTS)
        emit_B(nc, h, qf_d, kvp0, kvp1, None, wprojT, bproj, xg, o_x1)
    return nc


def emit_B(nc, h, qf_d, kvp0, kvp1, kvp_red, wprojT, bproj, xg, o_x1):
    with ExitStack() as st:
        qf = st.enter_context(h.sbt("qf_sb", [128, 8, TOK], BF16))
        x_tm = st.enter_context(h.sbt("x_tm", [128, NCH, C], BF16))
        kvE = st.enter_context(h.sbt("kvE", [128, 8, 65], BF16))
        wp = st.enter_context(h.sbt("wp", [128, 8, C], BF16))
        bp = st.enter_context(h.sbt("bp", [1, C], BF16))
        z_sb = st.enter_context(h.sbt("z_sb", [128, NCH, C], BF16))
        s1t = st.enter_context(h.sbt("s1t", [128, NCH], F32))
        s2t = st.enter_context(h.sbt("s2t", [128, NCH], F32))
        sc1t = st.enter_context(h.sbt("sc1t", [128, NCH], F32))
        sc2t = st.enter_context(h.sbt("sc2t", [128, NCH], F32))
        junk_a = st.enter_context(h.sbt("junk_a", [128, C], BF16))
        rec = [st.enter_context(h.sbt(f"rec{j}", [128, HEADS], F32)) for j in range(2)]
        radd = [st.enter_context(h.sbt(f"radd{j}", [128, HEADS], F32)) for j in range(2)]

        with h.sbt("kva", [128, 8 * 65], F32) as kva, \
                h.sbt("kvb", [128, 8 * 65], F32) as kvb:
            h.gdma(qf.ap(), qf_d.rearrange("(qc p) t -> p qc t", p=128))
            h.gdma(x_tm.ap(), xg.rearrange("(cc p) f -> p cc f", p=128))
            h.gdma(wp.ap(), wprojT.rearrange("(dc p) o -> p dc o", p=128))
            h.gdma(bp.ap(), bproj.rearrange("(a f) -> a f", a=1))
            if kvp_red is not None:
                h.gdma(kvE.ap().rearrange("p pr f -> p (pr f)"), kvp_red)
                h.flush()
            else:
                h.dma(kva.ap(), kvp0)
                h.dma(kvb.ap(), kvp1)
                h.flush()
                nc.vector.tensor_add(kvE.ap().rearrange("p pr f -> p (pr f)"),
                                     kva.ap(), kvb.ap())
                h.flush()

        with h.sbt("attn_tm", [128, NCH, C], BF16) as attn_tm, \
                h.sbt("attn_fm", [128, 8, TOK], BF16) as attn_fm:
            with h.pst("poA0", [128, 8, 128], F32) as poA0, \
                    h.pst("poB0", [128, 8, 128], F32) as poB0, \
                    h.pst("poA1", [128, 8, 128], F32) as poA1, \
                    h.pst("poB1", [128, 8, 128], F32) as poB1:
                gens = [(poA0, poB0), (poA1, poB1)]

                def mms(i):
                    poA, poB = gens[i % 2]
                    for pr in range(8):
                        for par, po in ((0, poA), (1, poB)):
                            hp = par * 64
                            ins = nc.tensor.matmul(po.ap()[:, pr, 0:65],
                                                   qf.ap()[hp:hp + 64, pr,
                                                           i * 128:(i + 1) * 128],
                                                   kvE.ap()[hp:hp + 64, pr, :],
                                                   start=True, stop=True,
                                                   tile_position=(hp, 0))
                    return ins

                def epi(cc):
                    poA, poB = gens[cc % 2]
                    for par, po in ((0, poA), (1, poB)):
                        ins = nc.vector.tensor_scalar(
                            radd[cc % 2].ap()[:, par * 8:(par + 1) * 8],
                            po.ap()[:, :, 64:65].squeeze(2),
                            ATT_EPS, 0.0, ALU.add, ALU.add)
                    h.vedge(ins)
                    ins = nc.vector.reciprocal(rec[cc % 2].ap(), radd[cc % 2].ap())
                    h.vedge(ins)
                    for par, po in ((0, poA), (1, poB)):
                        dstv = bass.AP(attn_tm.ap().tensor,
                                       attn_tm.ap().offset + cc * C + par * 64,
                                       [list(attn_tm.ap().ap[0]), [128, 8], [1, 64]])
                        ins = nc.vector.tensor_tensor(
                            dstv, po.ap()[:, :, 0:64],
                            _ap_bcast_last(rec[cc % 2].ap()[:, par * 8:(par + 1) * 8], 64),
                            ALU.mult)
                    return ins

                h.pipe(NCH, mms, [("vector", epi)], 2, name="oe")

            with h.pst("ps_ta", [128, 8, 128], BF16) as ps_ta, \
                    h.pst("ps_tb", [128, 8, 128], BF16) as ps_tb:
                h.pipe_transpose(
                    [([attn_tm.ap()[:, cc, dc * 128:(dc + 1) * 128] for dc in range(8)],
                      attn_fm.ap()[:, :, cc * 128:(cc + 1) * 128]) for cc in range(NCH)],
                    ps_ta, ps_tb)

            with h.pst("pz0", [128, C], F32) as pz0, \
                    h.pst("pz1", [128, C], F32) as pz1:
                pzs = [pz0, pz1]

                def mms2(i):
                    pz = pzs[i % 2]
                    for half in range(2):
                        sl = slice(half * 512, (half + 1) * 512)
                        for dc in range(8):
                            nc.tensor.matmul(pz.ap()[:, sl],
                                             attn_fm.ap()[:, dc, i * 128:(i + 1) * 128],
                                             wp.ap()[:, dc, sl],
                                             start=(dc == 0), stop=False)
                        nc.tensor.matmul(pz.ap()[:, sl], h.ones_row.ap(),
                                         bp.ap()[:, sl], start=False, stop=False)
                        ins = nc.tensor.matmul(pz.ap()[:, sl], h.ident.ap(),
                                               x_tm.ap()[:, i, sl],
                                               start=False, stop=True)
                    return ins

                def epi2(cc):
                    pz = pzs[cc % 2]
                    ins = nc.vector.tensor_scalar(z_sb.ap()[:, cc, :], pz.ap(),
                                                  1.0, 0.0, ALU.mult, ALU.add,
                                                  accum_out=s1t.ap()[:, cc:cc + 1])
                    h.vedge(ins)
                    return nc.vector.scalar_tensor_tensor(
                        junk_a.ap(), z_sb.ap()[:, cc, :], 1.0, z_sb.ap()[:, cc, :],
                        ALU.mult, ALU.mult, accum_out=s2t.ap()[:, cc:cc + 1])

                h.pipe(NCH, mms2, [("vector", epi2)], 2, name="pj")
            h.ln_batch_stats(s1t.ap(), s2t.ap(), sc1t.ap(), sc2t.ap(), NCH, C,
                             h.cc("ln_eps"))
            for cc in range(NCH):
                nc.vector.tensor_scalar(attn_tm.ap()[:, cc, :], z_sb.ap()[:, cc, :],
                                        sc1t.ap()[:, cc:cc + 1], sc2t.ap()[:, cc:cc + 1],
                                        ALU.subtract, ALU.mult)
            h.bar()
            h.dma(o_x1.rearrange("(cc p) f -> p cc f", p=128), attn_tm.ap())
            h.flush()


def build_C():
    nc = _nc()
    x1_d = nc.dram_tensor("x1", [TOK, C], BF16, kind="ExternalInput").ap()
    wgT = nc.dram_tensor("wgT", [C, FFN_H], I32, kind="ExternalInput").ap()
    wuT = nc.dram_tensor("wuT", [C, FFN_H], I32, kind="ExternalInput").ap()
    wdT = nc.dram_tensor("wdT", [FFN_H, C], I32, kind="ExternalInput").ap()
    o_x2 = nc.dram_tensor("x2", [TOK, C], BF16, kind="ExternalOutput").ap()
    with ExitStack() as gst:
        h = H(nc, gst)
        h.mk_ident()
        h.mk_consts(ALL_CONSTS)
        emit_C(nc, h, x1_d, wgT, wuT, wdT, o_x2, None)
    return nc


def emit_C(nc, h, x1_d, wgT, wuT, wdT, o_x2, halo_in):
    with ExitStack() as st:
        x1_tm = st.enter_context(h.sbt("x1_tm", [128, NCH, C], BF16))
        wd = st.enter_context(h.sbt("wd", [128, 24, C], BF16))
        hsb = st.enter_context(h.sbt("hsb", [128, 24, 1024], BF16))
        s1t = st.enter_context(h.sbt("s1t", [128, NCH], F32))
        s2t = st.enter_context(h.sbt("s2t", [128, NCH], F32))
        sc1t = st.enter_context(h.sbt("sc1t", [128, NCH], F32))
        sc2t = st.enter_context(h.sbt("sc2t", [128, NCH], F32))
        junk_a = st.enter_context(h.sbt("junk_a", [128, C], BF16))
        sg = [st.enter_context(h.sbt(f"sg{j}", [128, 512], F32)) for j in range(3)]

        h.gdma(x1_tm.ap(), x1_d.rearrange("(cc p) f -> p cc f", p=128))
        h.gdma(wd.ap(), wdT.rearrange("(hc p) d -> p hc d", p=128))
        h.flush()

        with h.sbt("x1f", [128, 8, 1024], BF16) as x1f, \
                h.sbt("zsb", [128, 8, C], BF16) as zsb:
            for th in range(2):
                ccs = list(range(th * 8, (th + 1) * 8))
                with h.pst(f"ps_ta_{th}", [128, 8, 128], BF16) as ps_ta, \
                        h.pst(f"ps_tb_{th}", [128, 8, 128], BF16) as ps_tb:
                    h.pipe_transpose(
                        [([x1_tm.ap()[:, cc, dc * 128:(dc + 1) * 128] for dc in range(8)],
                          x1f.ap()[:, :, i * 128:(i + 1) * 128])
                         for i, cc in enumerate(ccs)],
                        ps_ta, ps_tb)
                for sec in range(3):
                    with h.sbt(f"wg_{th}_{sec}", [128, 8, 1024], BF16) as wg_sb, \
                            h.sbt(f"wu_{th}_{sec}", [128, 8, 1024], BF16) as wu_sb, \
                            h.pst(f"pg_{th}_{sec}0", [128, 512], F32) as pg0, \
                            h.pst(f"pg_{th}_{sec}1", [128, 512], F32) as pg1, \
                            h.pst(f"pg_{th}_{sec}2", [128, 512], F32) as pg2, \
                            h.pst(f"pu_{th}_{sec}0", [128, 512], F32) as pu0, \
                            h.pst(f"pu_{th}_{sec}1", [128, 512], F32) as pu1, \
                            h.pst(f"pu_{th}_{sec}2", [128, 512], F32) as pu2:
                        h.gdma(wg_sb.ap(), wgT.rearrange("(dc p) e -> p dc e", p=128)
                               [:, :, sec * 1024:(sec + 1) * 1024])
                        h.gdma(wu_sb.ap(), wuT.rearrange("(dc p) e -> p dc e", p=128)
                               [:, :, sec * 1024:(sec + 1) * 1024])
                        h.flush()
                        pgs, pus = [pg0, pg1, pg2], [pu0, pu1, pu2]
                        units = [(hc, ts) for hc in range(8) for ts in range(2)]

                        def mms(i, sec=sec, wg_sb=wg_sb, wu_sb=wu_sb):
                            hc, ts = units[i]
                            tsl = slice(ts * 512, (ts + 1) * 512)
                            for dc in range(8):
                                nc.tensor.matmul(pgs[i % 3].ap(),
                                                 wg_sb.ap()[:, dc, hc * 128:(hc + 1) * 128],
                                                 x1f.ap()[:, dc, tsl],
                                                 start=(dc == 0), stop=(dc == 7))
                            for dc in range(8):
                                ins = nc.tensor.matmul(
                                    pus[i % 3].ap(),
                                    wu_sb.ap()[:, dc, hc * 128:(hc + 1) * 128],
                                    x1f.ap()[:, dc, tsl],
                                    start=(dc == 0), stop=(dc == 7))
                            return ins

                        def st1(u):
                            return nc.scalar.activation(sg[u % 3].ap(), pgs[u % 3].ap(),
                                                        AF.Silu, scale=h.cc("qs"))

                        def st2(u, sec=sec):
                            hc, ts = units[u]
                            tsl = slice(ts * 512, (ts + 1) * 512)
                            return nc.vector.scalar_tensor_tensor(
                                hsb.ap()[:, sec * 8 + hc, tsl], pus[u % 3].ap(),
                                QS * QS, sg[u % 3].ap(), ALU.mult, ALU.mult)

                        h.pipe(len(units), mms, [("scalar", st1), ("vector", st2)], 3, name="qkv")
                with h.pst(f"pd_{th}0", [128, C], F32) as pd0, \
                        h.pst(f"pd_{th}1", [128, C], F32) as pd1:
                    pds = [pd0, pd1]

                    def mms3(i, ccs=ccs):
                        cc = ccs[i]
                        pd = pds[i % 2]
                        for half in range(2):
                            sl = slice(half * 512, (half + 1) * 512)
                            for hc in range(24):
                                nc.tensor.matmul(pd.ap()[:, sl],
                                                 hsb.ap()[:, hc, i * 128:(i + 1) * 128],
                                                 wd.ap()[:, hc, sl],
                                                 start=(hc == 0), stop=False)
                            ins = nc.tensor.matmul(pd.ap()[:, sl], h.ident.ap(),
                                                   x1_tm.ap()[:, cc, sl],
                                                   start=False, stop=True)
                        return ins

                    def epi3(u, ccs=ccs):
                        cc = ccs[u]
                        pd = pds[u % 2]
                        ins = nc.vector.tensor_scalar(
                            zsb.ap()[:, u, :], pd.ap(), 1.0, 0.0,
                            ALU.mult, ALU.add, accum_out=s1t.ap()[:, cc:cc + 1])
                        h.vedge(ins)
                        return nc.vector.scalar_tensor_tensor(
                            junk_a.ap(), zsb.ap()[:, u, :], 1.0,
                            zsb.ap()[:, u, :], ALU.mult, ALU.mult,
                            accum_out=s2t.ap()[:, cc:cc + 1])

                    h.pipe(8, mms3, [("vector", epi3)], 2, name="dn")
                h.ln_batch_stats(s1t.ap()[:, th * 8:(th + 1) * 8],
                                 s2t.ap()[:, th * 8:(th + 1) * 8],
                                 sc1t.ap()[:, th * 8:(th + 1) * 8],
                                 sc2t.ap()[:, th * 8:(th + 1) * 8], 8, C,
                                 h.cc("ln_eps"))
                for i, cc in enumerate(ccs):
                    nc.vector.tensor_scalar(x1_tm.ap()[:, cc, :], zsb.ap()[:, i, :],
                                            sc1t.ap()[:, cc:cc + 1],
                                            sc2t.ap()[:, cc:cc + 1],
                                            ALU.subtract, ALU.mult)
                h.bar()
        h.dma(o_x2.rearrange("(cc p) f -> p cc f", p=128), x1_tm.ap())
        if halo_in is not None:
            h.dma(halo_in, x1_tm.ap()[:, NCH - 1, :])
        h.flush()


NCH_IN = 17
DBG_D = False


def build_D():
    nc = _nc()
    x2in = nc.dram_tensor("x2in", [NCH_IN * 128, C], BF16, kind="ExternalInput").ap()
    wqkvT = nc.dram_tensor("wqkvT", [C, 3 * C], F32, kind="ExternalInput").ap()
    lnw = nc.dram_tensor("lnw", [C], F32, kind="ExternalInput").ap()
    lnb = nc.dram_tensor("lnb", [C], F32, kind="ExternalInput").ap()
    woutT = nc.dram_tensor("woutT", [C, C], F32, kind="ExternalInput").ap()
    cost = nc.dram_tensor("cost", [NCH_IN * 128, 32], F32, kind="ExternalInput").ap()
    ssint = nc.dram_tensor("ssint", [NCH_IN * 128, 32], F32, kind="ExternalInput").ap()
    mown = nc.dram_tensor("mown", [128, 160], F32, kind="ExternalInput").ap()
    mhalo = nc.dram_tensor("mhalo", [128, 160], F32, kind="ExternalInput").ap()
    o_x3 = nc.dram_tensor("x3", [TOK, C], BF16, kind="ExternalOutput").ap()
    with ExitStack() as gst:
        h = H(nc, gst)
        h.mk_ident()
        h.mk_consts(ALL_CONSTS)
        emit_D(nc, h, x2in, None, None, wqkvT, lnw, lnb, woutT, cost, ssint,
               mown, mhalo, o_x3)
    return nc


def emit_D(nc, h, x2in, x2_own, halo_out, wqkvT, lnw, lnb, woutT, cost, ssint,
           mown, mhalo, o_x3):
    with ExitStack() as st:
        x2_tm = st.enter_context(h.sbt("x2_tm", [128, NCH_IN, C], BF16))
        ct = st.enter_context(h.sbt("ct", [128, NCH_IN, 32], F32))
        sst = st.enter_context(h.sbt("sst", [128, NCH_IN, 32], F32))
        gam = st.enter_context(h.sbt("gam", [128, 8], F32))
        s1t = st.enter_context(h.sbt("s1t", [128, NCH_IN], F32))
        s2t = st.enter_context(h.sbt("s2t", [128, NCH_IN], F32))
        sc1t = st.enter_context(h.sbt("sc1t", [128, NCH_IN], F32))
        sc2t = st.enter_context(h.sbt("sc2t", [128, NCH_IN], F32))
        rec = st.enter_context(h.sbt("rec", [128, HEADS], F32))
        radd = st.enter_context(h.sbt("radd", [128, HEADS], F32))
        atrow = [st.enter_context(h.sbt(f"atrow{j}", [128, C], BF16)) for j in range(2)]
        junk_a = atrow[0]
        st2 = ExitStack()
        qfm = st2.enter_context(h.sbt("qfm", [128, 8, NCH_IN * 128], BF16))
        kfm = st2.enter_context(h.sbt("kfm", [128, 8, NCH_IN * 128], BF16))
        vex = st2.enter_context(h.sbt("vex", [128, NCH_IN, HEADS * 65], BF16))

        if x2in is not None:
            h.gdma(x2_tm.ap(), x2in.rearrange("(cc p) f -> p cc f", p=128))
        else:
            h.gdma(x2_tm.ap()[:, 0, :],
                   halo_out.rearrange("(cc p) f -> p cc f", p=128)[:, 0, :])
            h.gdma(x2_tm.ap()[:, 1:NCH_IN, :],
                   x2_own.rearrange("(cc p) f -> p cc f", p=128))
        h.gdma(ct.ap(), cost.rearrange("(cc p) f -> p cc f", p=128))
        h.gdma(sst.ap(), ssint.rearrange("(cc p) f -> p cc f", p=128))
        h.flush()
        nc.vector.memset(vex.ap().rearrange("p cc (hd f) -> p cc hd f", f=65)[:, :, :, 64:65], 1.0)
        for cc in range(NCH_IN):
            ins = nc.vector.tensor_scalar(junk_a.ap(), x2_tm.ap()[:, cc, :], 1.0, 0.0,
                                          ALU.mult, ALU.add,
                                          accum_out=s1t.ap()[:, cc:cc + 1])
            h.vedge(ins)
            nc.vector.scalar_tensor_tensor(junk_a.ap(), x2_tm.ap()[:, cc, :], 1.0,
                                           x2_tm.ap()[:, cc, :], ALU.mult, ALU.mult,
                                           accum_out=s2t.ap()[:, cc:cc + 1])
            h.vedge(nc.vector.engine_nop())
        h.bar()
        h.ln_batch_stats(s1t.ap(), s2t.ap(), sc1t.ap(), sc2t.ap(), NCH_IN, C,
                         h.cc("ln_eps"))

        with h.sbt("wq", [128, 8, 3 * C], BF16) as wq, \
                h.sbt("bq", [1, 3 * C], BF16) as bq, \
                h.sbt("bet", [128, 8], BF16) as bet:
            h.gdma(wq.ap(), wqkvT.rearrange("(dc p) e -> p dc e", p=128))
            with nc.allow_non_contiguous_dma(reason="tiny 1k-element vector loads"):
                h.gdma(gam.ap(), lnw.rearrange("(dc p) -> p dc", p=128))
                h.gdma(bet.ap(), lnb.rearrange("(dc p) -> p dc", p=128))
            h.flush()
            with h.pst("pb", [1, 3 * C], F32) as pb:
                for dc in range(8):
                    for ns in range(6):
                        nc.tensor.matmul(pb.ap()[:, ns * 512:(ns + 1) * 512],
                                         bet.ap()[:, dc:dc + 1],
                                         wq.ap()[:, dc, ns * 512:(ns + 1) * 512],
                                         start=(dc == 0), stop=(dc == 7))
                h.bar()
                nc.vector.tensor_copy(bq.ap(), pb.ap())
                h.bar()
            for dc in range(8):
                nc.vector.tensor_scalar_mul(wq.ap()[:, dc, :], wq.ap()[:, dc, :],
                                            gam.ap()[:, dc:dc + 1])
            h.bar()

            with h.sbt("xn_c", [128, C], BF16) as xn_c, \
                    h.sbt("xn_f", [128, 8, 128], BF16) as xn_f, \
                    h.sbt("rr", [128, 512], BF16) as rr, \
                    h.sbt("rt", [128, 512], BF16) as rt, \
                    h.pst("ps_ta", [128, 8, 128], BF16) as ps_ta, \
                    h.pst("ps_tb", [128, 8, 128], BF16) as ps_tb, \
                    h.pst("ps_q0", [128, 512], F32) as ps_q0, \
                    h.pst("ps_q1", [128, 512], F32) as ps_q1:
                class _V:
                    def __init__(self, a):
                        self._a = a

                    def ap(self):
                        return self._a
                rds = [_V(atrow[1].ap()[:, 0:512]), _V(atrow[1].ap()[:, 512:1024])]
                pqs = [ps_q0, ps_q1]
                pts = [ps_ta, ps_tb]
                for cc in range(NCH_IN):
                    nc.vector.tensor_scalar(xn_c.ap(), x2_tm.ap()[:, cc, :],
                                            sc1t.ap()[:, cc:cc + 1],
                                            sc2t.ap()[:, cc:cc + 1],
                                            ALU.subtract, ALU.mult)
                    h.bar()
                    h.pipe_transpose(
                        [([xn_c.ap()[:, dc * 128:(dc + 1) * 128] for dc in range(4)],
                          xn_f.ap()[:, 0:4, :]),
                         ([xn_c.ap()[:, dc * 128:(dc + 1) * 128] for dc in range(4, 8)],
                          xn_f.ap()[:, 4:8, :])],
                        ps_ta, ps_tb)
                    for i in range(6 + 3):
                        if i < 6:
                            es = i
                            esl = slice(es * 512, (es + 1) * 512)
                            for dc in range(8):
                                nc.tensor.matmul(pqs[i % 2].ap(), xn_f.ap()[:, dc, :],
                                                 wq.ap()[:, dc, esl],
                                                 start=(dc == 0), stop=False)
                            nc.tensor.matmul(pqs[i % 2].ap(), h.ones_row.ap(),
                                             bq.ap()[:, esl], start=False, stop=True)
                        if 1 <= i <= 6:
                            es = i - 1
                            pq = pqs[es % 2]
                            if es < 4:
                                ca = ct.ap()[:, cc, :]
                                sa = sst.ap()[:, cc, :]
                                cb = bass.AP(ca.tensor, ca.offset,
                                             [list(ca.ap[0])] + [[0, 8], [0, 2], [1, 32]])
                                sb = bass.AP(sa.tensor, sa.offset,
                                             [list(sa.ap[0])] + [[0, 8], [1, 32]])
                                qv = pq.ap().rearrange("p (hd two f) -> p hd two f",
                                                       two=2, f=32)
                                rr2 = rr.ap().rearrange("p (hd two f) -> p hd two f",
                                                        two=2, f=32)
                                rt2 = rt.ap().rearrange("p (hd two f) -> p hd two f",
                                                        two=2, f=32)
                                nc.vector.tensor_tensor(rr2, qv, cb, ALU.mult)
                                nc.vector.tensor_tensor(rt2[:, :, 0, :], qv[:, :, 1, :],
                                                        sb, ALU.mult)
                                ins = nc.vector.tensor_tensor(rt2[:, :, 1, :],
                                                              qv[:, :, 0, :], sb,
                                                              ALU.mult)
                                h.vedge(ins)
                                d2 = rds[es % 2].ap().rearrange(
                                    "p (hd two f) -> p hd two f", two=2, f=32)
                                nc.vector.tensor_tensor(d2[:, :, 0, :], rr2[:, :, 0, :],
                                                        rt2[:, :, 0, :], ALU.subtract)
                                nc.vector.tensor_tensor(d2[:, :, 1, :], rr2[:, :, 1, :],
                                                        rt2[:, :, 1, :], ALU.add)
                            else:
                                nc.scalar.copy(
                                    vex.ap().rearrange("p cc2 (hd f) -> p cc2 hd f", f=65)
                                    [:, cc, (es - 4) * 8:(es - 4) * 8 + 8, 0:64],
                                    pq.ap().rearrange("p (hd f) -> p hd f", f=64))
                        if 2 <= i <= 7:
                            es = i - 2
                            if es < 4:
                                for dc4 in range(4):
                                    nc.tensor.transpose(
                                        pts[es % 2].ap()[:, dc4, :],
                                        rds[es % 2].ap()[:, dc4 * 128:(dc4 + 1) * 128],
                                        h.ident.ap())
                        if 3 <= i <= 8:
                            es = i - 3
                            if es < 4:
                                tgt = qfm if es < 2 else kfm
                                base = (es % 2) * 4
                                nc.scalar.copy(
                                    tgt.ap()[:, base:base + 4, cc * 128:(cc + 1) * 128],
                                    pts[es % 2].ap()[:, 0:4, :])
                        h.bar()

        attn_d = nc.dram_tensor(h.prefix + "attn_d", [TOK, C], BF16, kind="Internal").ap()
        with h.sbt("m_own", [128, 160], F32) as m_own, \
                h.sbt("m_halo", [128, 160], F32) as m_halo, \
                h.pst("ps_sA", [128, 4, 256], F32) as ps_sA, \
                h.pst("ps_sB", [128, 4, 256], F32) as ps_sB, \
                h.pst("po", [128, HEADS, 128], F32) as po, \
                h.sbt("p_a", [128, HEADS, 160], BF16) as p_a, \
                h.sbt("p_b", [128, HEADS, 160], BF16) as p_b:
            h.dma(m_own.ap(), mown)
            h.dma(m_halo.ap(), mhalo)
            h.flush()
            pbufs = [p_a, p_b]
            for kc in range(NCH_IN):
                p_cur = pbufs[kc % 2]
                p_prev = pbufs[1 - kc % 2]
                qlo = kc * 128
                qn = min(160, NCH_IN * 128 - qlo)
                mask = m_halo if kc == 0 else m_own
                for hh in range(2):
                    for i4 in range(4):
                        for par, pss in ((0, ps_sA), (1, ps_sB)):
                            hd = hh * 8 + 2 * i4 + par
                            hp = par * 64
                            nc.tensor.matmul(pss.ap()[:, i4, 0:qn],
                                             kfm.ap()[hp:hp + 64, hd // 2, kc * 128:(kc + 1) * 128],
                                             qfm.ap()[hp:hp + 64, hd // 2, qlo:qlo + qn],
                                             start=True, stop=True,
                                             tile_position=(hp, 0))
                    h.bar()
                    for pss in (ps_sA, ps_sB):
                        nc.scalar.activation(pss.ap()[:, :, 0:qn], pss.ap()[:, :, 0:qn],
                                             AF.Exp, bias=h.cc("negC"), scale=h.cc("s8"))
                    h.bar()
                    for par, pss in ((0, ps_sA), (1, ps_sB)):
                        pv = p_cur.ap()[:, hh * 8 + par:hh * 8 + 8:2, 0:qn]
                        nc.vector.tensor_tensor(pv, pss.ap()[:, :, 0:qn],
                                                _ap_bcast(mask.ap()[:, 0:qn], 4),
                                                ALU.mult)
                    h.bar()
                if kc == 0:
                    continue
                oc = kc
                for hd in range(HEADS):
                    nc.tensor.matmul(po.ap()[:, hd, 0:65],
                                     p_cur.ap()[:, hd, 0:128],
                                     vex.ap().rearrange("p cc2 (h2 f) -> p cc2 h2 f", f=65)
                                     [:, kc, hd, :],
                                     start=True, stop=False)
                    nc.tensor.matmul(po.ap()[0:32, hd, 0:65],
                                     p_prev.ap()[:, hd, 128:160],
                                     vex.ap().rearrange("p cc2 (h2 f) -> p cc2 h2 f", f=65)
                                     [:, kc - 1, hd, :],
                                     start=False, stop=True)
                h.bar()
                ins = nc.vector.tensor_scalar(radd.ap(), po.ap()[:, :, 64:65].squeeze(2),
                                              1e-30, 0.0, ALU.add, ALU.add)
                h.vedge(ins)
                ins = nc.vector.reciprocal(rec.ap(), radd.ap())
                h.vedge(ins)
                nc.vector.tensor_tensor(
                    atrow[oc % 2].ap().rearrange("p (hd f) -> p hd f", f=64),
                    po.ap()[:, :, 0:64], _ap_bcast_last(rec.ap(), 64), ALU.mult)
                h.bar()
                h.dma(attn_d.rearrange("(c2 p) f -> p c2 f", p=128)[:, oc - 1, :],
                      atrow[oc % 2].ap())
                if oc >= 2:
                    h.dwait()

        st2.close()
        with h.sbt("attn_fm", [128, 8, TOK], BF16) as attn_fm, \
                h.sbt("wo", [128, 8, C], BF16) as wo, \
                h.sbt("z_sb", [128, NCH, C], BF16) as z_sb, \
                h.pst("ps_ta3", [128, 8, 128], BF16) as ps_ta3, \
                h.pst("ps_tb3", [128, 8, 128], BF16) as ps_tb3, \
                h.pst("pz0", [128, C], F32) as pz0, \
                h.pst("pz1", [128, C], F32) as pz1:
            h.gdma(wo.ap(), woutT.rearrange("(dc p) o -> p dc o", p=128))
            for cc in range(NCH):
                h.dma(atrow[cc % 2].ap(),
                      attn_d.rearrange("(c2 p) f -> p c2 f", p=128)[:, cc, :])
                h.flush()
                h.pipe_transpose(
                    [([atrow[cc % 2].ap()[:, dc * 128:(dc + 1) * 128] for dc in range(4)],
                      attn_fm.ap()[:, 0:4, cc * 128:(cc + 1) * 128]),
                     ([atrow[cc % 2].ap()[:, dc * 128:(dc + 1) * 128] for dc in range(4, 8)],
                      attn_fm.ap()[:, 4:8, cc * 128:(cc + 1) * 128])],
                    ps_ta3, ps_tb3)
            pzs = [pz0, pz1]
            for i in range(NCH + 1):
                if i < NCH:
                    pz = pzs[i % 2]
                    for half in range(2):
                        sl = slice(half * 512, (half + 1) * 512)
                        for dc in range(8):
                            nc.tensor.matmul(pz.ap()[:, sl],
                                             attn_fm.ap()[:, dc, i * 128:(i + 1) * 128],
                                             wo.ap()[:, dc, sl],
                                             start=(dc == 0), stop=False)
                        nc.tensor.matmul(pz.ap()[:, sl], h.ident2.ap(),
                                         x2_tm.ap()[:, i + 1, sl], start=False, stop=True)
                if i > 0:
                    cc = i - 1
                    pz = pzs[cc % 2]
                    ins = nc.vector.tensor_scalar(z_sb.ap()[:, cc, :], pz.ap(),
                                                  1.0, 0.0, ALU.mult, ALU.add,
                                                  accum_out=s1t.ap()[:, cc:cc + 1])
                    h.vedge(ins)
                    nc.vector.scalar_tensor_tensor(junk_a.ap(), z_sb.ap()[:, cc, :],
                                                   1.0, z_sb.ap()[:, cc, :],
                                                   ALU.mult, ALU.mult,
                                                   accum_out=s2t.ap()[:, cc:cc + 1])
                h.bar()
            h.ln_batch_stats(s1t.ap(), s2t.ap(), sc1t.ap(), sc2t.ap(), NCH, C,
                             h.cc("ln_eps"))
            for cc in range(NCH):
                nc.vector.tensor_scalar(z_sb.ap()[:, cc, :].bitcast(BF16),
                                        z_sb.ap()[:, cc, :],
                                        sc1t.ap()[:, cc:cc + 1], sc2t.ap()[:, cc:cc + 1],
                                        ALU.subtract, ALU.mult)
            h.bar()
            h.dma(o_x3.rearrange("(cc p) f -> p cc f", p=128), z_sb.ap())
            h.flush()


GRP = 2


def build_E():
    nc = _nc()
    x3f = nc.dram_tensor("x3f", [T, C], BF16, kind="ExternalInput").ap()
    wq_d = nc.dram_tensor("wq", [C, 512], F32, kind="ExternalInput").ap()
    wk_d = nc.dram_tensor("wk", [C, 512], F32, kind="ExternalInput").ap()
    wv_d = nc.dram_tensor("wv", [C, 512], F32, kind="ExternalInput").ap()
    o_at = nc.dram_tensor("attn", [512, T], BF16, kind="ExternalOutput").ap()
    with ExitStack() as gst:
        h = H(nc, gst)
        h.mk_ident()
        h.mk_consts(ALL_CONSTS)
        emit_E(nc, h, x3f, wq_d, wk_d, wv_d, o_at, a2a_mode=False)
    return nc


def emit_E(nc, h, x3f, wq_d, wk_d, wv_d, o_at, a2a_mode):
    with ExitStack() as st:
        qf = st.enter_context(h.sbt("qf", [128, 4, T], BF16))
        kf = st.enter_context(h.sbt("kf", [128, 4, T], BF16))
        vt = st.enter_context(h.sbt("vt", [128, 32, 512], BF16))

        with h.sbt("wqs", [128, 8, 512], BF16) as wqs, \
                h.sbt("wks", [128, 8, 512], BF16) as wks, \
                h.sbt("wvs", [128, 8, 512], BF16) as wvs:
            h.gdma(wqs.ap(), wq_d.rearrange("(dc p) e -> p dc e", p=128))
            h.gdma(wks.ap(), wk_d.rearrange("(dc p) e -> p dc e", p=128))
            h.gdma(wvs.ap(), wv_d.rearrange("(dc p) e -> p dc e", p=128))
            h.flush()
            for th in range(4):
                with h.sbt(f"x_tm_{th}", [128, 8, C], BF16) as x_tm, \
                        h.sbt(f"x_fm_{th}", [128, 8, 1024], BF16) as x_fm, \
                        h.pst(f"ps_t_{th}", [128, 8, 128], BF16) as ps_t, \
                        h.pst(f"ps_m_{th}", [128, 512], F32) as ps_m:
                    h.gdma(x_tm.ap(), x3f.rearrange("(cc p) f -> p cc f", p=128)
                           [:, th * 8:(th + 1) * 8, :])
                    h.flush()
                    for cc in range(8):
                        h.transpose_group(
                            ps_t,
                            [x_tm.ap()[:, cc, dc * 128:(dc + 1) * 128] for dc in range(8)],
                            x_fm.ap()[:, :, cc * 128:(cc + 1) * 128])
                    units = []
                    for lc in range(4):
                        for ts in range(2):
                            for w_sb, tgt in ((wqs, qf), (wks, kf)):
                                units.append((
                                    [w_sb.ap()[:, dc, lc * 128:(lc + 1) * 128]
                                     for dc in range(8)],
                                    [x_fm.ap()[:, dc, ts * 512:(ts + 1) * 512]
                                     for dc in range(8)],
                                    tgt.ap()[:, lc, th * 1024 + ts * 512:
                                             th * 1024 + (ts + 1) * 512]))
                    for cc in range(8):
                        units.append((
                            [x_fm.ap()[:, dc, cc * 128:(cc + 1) * 128] for dc in range(8)],
                            [wvs.ap()[:, dc, :] for dc in range(8)],
                            vt.ap()[:, th * 8 + cc, :]))
                    with h.pst(f"ps_m2_{th}", [128, 512], F32) as ps_m2:
                        pms = [ps_m, ps_m2]
                        n = len(units)
                        for i in range(n + 1):
                            if i < n:
                                ls, rs, _ = units[i]
                                for dc in range(8):
                                    nc.tensor.matmul(pms[i % 2].ap(), ls[dc], rs[dc],
                                                     start=(dc == 0), stop=(dc == 7))
                            if i > 0:
                                _, _, dst = units[i - 1]
                                nc.scalar.copy(dst, pms[(i - 1) % 2].ap())
                            h.bar()

        acc_a = st.enter_context(h.sbt("acc_a", [128, T], F32))
        acc_b = st.enter_context(h.sbt("acc_b", [128, T], F32))
        at_sb = st.enter_context(h.sbt("at_sb", [128, T], BF16))
        zp = st.enter_context(h.sbt("zp", [128, 2, GRP, 8], F32))
        zs = st.enter_context(h.sbt("zs", [128, 2, GRP, 2], F32))
        rz = st.enter_context(h.sbt("rz", [128, 2, GRP, 2], F32))
        rzb = st.enter_context(h.sbt("rzb", [128, 2, GRP, 2], BF16))
        with h.sbt("p_sb", [128, 2, GRP, 2, T], BF16) as p_sb, \
                h.sbt("w_sb", [128, 2, GRP, 128], BF16) as w_sb, \
                h.pst("ps_s0", [128, 1024], F32) as ps_s0, \
                h.pst("ps_s1", [128, 1024], F32) as ps_s1, \
                h.pst("ps_o0", [128, 512], F32) as ps_o0, \
                h.pst("ps_o1", [128, 512], F32) as ps_o1:
            smm, sexp = h.sem("smm"), h.sem("sexp")
            somm, soac = h.sem("somm"), h.sem("soac")
            swp = h.sem("swp")
            sbufs = [ps_s0, ps_s1]
            obufs = [ps_o0, ps_o1]
            n_mm = n_exp = n_omm = n_oac = n_wp = 0
            n_g = 32 // GRP
            exp_done = {}

            for pair in range(4):
                for g in range(n_g + 1):
                    if g > 0:
                        gp = g - 1
                        gg = gp % 2
                        nc.vector.wait_ge(sexp, exp_done[(pair, gp)])
                        for si in range(GRP):
                            ins = nc.vector.reduce_sum(
                                zs.ap()[:, gg, si, :],
                                zp.ap()[:, gg, si, :].rearrange(
                                    "p (h2 f) -> p h2 f", f=4), axis=AX.X)
                        h.vedge(ins)
                        ins = nc.vector.reciprocal(
                            rz.ap()[:, gg].rearrange("p a b -> p (a b)"),
                            zs.ap()[:, gg].rearrange("p a b -> p (a b)"))
                        h.vedge(ins)
                        ins = nc.vector.tensor_scalar(
                            rzb.ap()[:, gg].rearrange("p a b -> p (a b)"),
                            rz.ap()[:, gg].rearrange("p a b -> p (a b)"),
                            1.0, 0.0, ALU.mult, ALU.add)
                        h.vedge(ins)
                        for si in range(GRP):
                            s = gp * GRP + si
                            ins = nc.vector.tensor_tensor(
                                w_sb.ap()[:, gg, si, :].rearrange(
                                    "p (h2 f) -> p h2 f", f=64),
                                vt.ap()[:, s, pair * 128:(pair + 1) * 128]
                                .rearrange("p (h2 f) -> p h2 f", f=64),
                                _ap_bcast_last(rzb.ap()[:, gg, si, :], 64), ALU.mult)
                        ins.then_inc(swp, 1)
                        n_wp += 1
                        nc.tensor.wait_ge(swp, n_wp)

                    def emit_out_qs(gp, qs):
                        nonlocal n_omm, n_oac
                        gg = gp % 2
                        obuf = obufs[n_omm % 2]
                        if n_omm >= 2:
                            nc.tensor.wait_ge(soac, n_omm - 1)
                        for si in range(GRP):
                            nc.tensor.matmul(obuf.ap()[0:64, :],
                                             w_sb.ap()[:, gg, si, 0:64],
                                             p_sb.ap()[:, gg, si, 0,
                                                       qs * 512:(qs + 1) * 512],
                                             start=(si == 0), stop=False,
                                             tile_position=(0, 0))
                            ins = nc.tensor.matmul(
                                obuf.ap()[64:128, :],
                                w_sb.ap()[:, gg, si, 64:128],
                                p_sb.ap()[:, gg, si, 1, qs * 512:(qs + 1) * 512],
                                start=(si == 0), stop=(si == GRP - 1),
                                tile_position=(0, 64))
                        ins.then_inc(somm, 1)
                        n_omm += 1
                        nc.vector.wait_ge(somm, n_omm)
                        qsl = slice(qs * 512, (qs + 1) * 512)
                        if gp == 0:
                            ins = nc.vector.tensor_scalar(
                                acc_a.ap()[:, qsl], obuf.ap(),
                                1.0, 0.0, ALU.mult, ALU.add)
                        elif gp % 2 == 1:
                            ins = nc.vector.tensor_add(acc_b.ap()[:, qsl],
                                                       obuf.ap(),
                                                       acc_a.ap()[:, qsl])
                        else:
                            ins = nc.vector.tensor_add(acc_a.ap()[:, qsl],
                                                       obuf.ap(),
                                                       acc_b.ap()[:, qsl])
                        ins.then_inc(soac, 1)
                        n_oac += 1

                    if g < n_g:
                        gg = g % 2
                        rnd = 0
                        for si in range(GRP):
                            s = g * GRP + si
                            for hi in range(2):
                                hp = hi * 64
                                for qp in range(4):
                                    buf = sbufs[n_mm % 2]
                                    if n_mm >= 2:
                                        nc.tensor.wait_ge(sexp, n_mm - 1)
                                    for half in range(2):
                                        q0 = (qp * 2 + half) * 512
                                        ins = nc.tensor.matmul(
                                            buf.ap()[:, half * 512:(half + 1) * 512],
                                            kf.ap()[hp:hp + 64, pair, s * 128:(s + 1) * 128],
                                            qf.ap()[hp:hp + 64, pair, q0:q0 + 512],
                                            start=True, stop=True,
                                            tile_position=(hp, 0))
                                    ins.then_inc(smm, 1)
                                    n_mm += 1
                                    nc.scalar.wait_ge(smm, n_mm)
                                    nc.scalar.activation(
                                        p_sb.ap()[:, gg, si, hi, qp * 1024:(qp + 1) * 1024],
                                        buf.ap(), AF.Exp, scale=h.cc("s32"),
                                        accum_out=zp.ap()[:, gg, si, hi * 4 + qp]
                                        .unsqueeze(1)).then_inc(sexp, 1)
                                    n_exp += 1
                                    if g > 0 and rnd % 2 == 0:
                                        emit_out_qs(g - 1, rnd // 2)
                                    rnd += 1
                        exp_done[(pair, g)] = n_exp
                    else:
                        for qs in range(8):
                            emit_out_qs(g - 1, qs)
                fin = acc_a if (n_g - 1) % 2 == 0 else acc_b
                h.bar()
                nc.vector.tensor_scalar(at_sb.ap(), fin.ap(), 1.0, 0.0, ALU.mult, ALU.add)
                h.bar()
                if a2a_mode:
                    av = o_at.rearrange("(s2 pr p) t -> p s2 pr t", s2=2, p=128)
                    for s2 in range(2):
                        h.dma(av[:, s2, pair, :], at_sb.ap()[:, s2 * TOK:(s2 + 1) * TOK])
                else:
                    h.dma(o_at.rearrange("(pr p) t -> p pr t", p=128)[:, pair, :],
                          at_sb.ap())
                h.flush()
        h.flush()


def build_F():
    nc = _nc()
    at_d = nc.dram_tensor("attn_own", [C, TOK], BF16, kind="ExternalInput").ap()
    x3_d = nc.dram_tensor("x3own", [TOK, C], BF16, kind="ExternalInput").ap()
    wo_d = nc.dram_tensor("wo", [C, C], F32, kind="ExternalInput").ap()
    wff_d = nc.dram_tensor("wff", [C, C], F32, kind="ExternalInput").ap()
    o_x4 = nc.dram_tensor("x4", [TOK, C], F32, kind="ExternalOutput").ap()
    with ExitStack() as gst:
        h = H(nc, gst)
        h.mk_ident()
        h.mk_consts(ALL_CONSTS)
        emit_F(nc, h, at_d, x3_d, wo_d, wff_d, o_x4)
    return nc


def emit_F(nc, h, at_d, x3_d, wo_d, wff_d, o_x4, sel=None):
    with ExitStack() as st:
        x3_tm = st.enter_context(h.sbt("x3_tm", [128, NCH, C], BF16))
        wo = st.enter_context(h.sbt("wo_sb", [128, 8, C], BF16))
        wff = st.enter_context(h.sbt("wff_sb", [128, 8, C], BF16))
        zA = st.enter_context(h.sbt("zA", [128, NCH, C], BF16))
        zB = st.enter_context(h.sbt("zB", [128, NCH, C], BF16))
        s1t = st.enter_context(h.sbt("s1t", [128, NCH], F32))
        s2t = st.enter_context(h.sbt("s2t", [128, NCH], F32))
        sc1t = st.enter_context(h.sbt("sc1t", [128, NCH], F32))
        sc2t = st.enter_context(h.sbt("sc2t", [128, NCH], F32))
        r1t = st.enter_context(h.sbt("r1t", [128, NCH], F32))
        r2t = st.enter_context(h.sbt("r2t", [128, NCH], F32))
        junk_a = st.enter_context(h.sbt("junk_a", [128, C], BF16))

        h.gdma(x3_tm.ap(), x3_d.rearrange("(cc p) f -> p cc f", p=128))
        h.gdma(wo.ap(), wo_d.rearrange("(lc p) d -> p lc d", p=128))
        h.gdma(wff.ap(), wff_d.rearrange("(dc p) d -> p dc d", p=128))
        h.flush()

        def l2_batch(s2, rt, n):
            nc.scalar.activation(s2[:, 0:n], s2[:, 0:n], AF.Sqrt)
            h.bar()
            nc.vector.tensor_scalar_max(s2[:, 0:n], s2[:, 0:n], L2_EPS)
            h.bar()
            nc.vector.reciprocal(rt[:, 0:n], s2[:, 0:n])
            h.bar()

        def zstats(dst_ap, ps, s1col, s2col, with_s1):
            if with_s1:
                ins = nc.vector.tensor_scalar(dst_ap, ps.ap(), 1.0, 0.0,
                                              ALU.mult, ALU.add, accum_out=s1col)
            else:
                ins = nc.vector.tensor_scalar(dst_ap, ps.ap(), 1.0, 0.0,
                                              ALU.mult, ALU.add)
            h.vedge(ins)
            nc.vector.scalar_tensor_tensor(junk_a.ap(), dst_ap, 1.0, dst_ap,
                                           ALU.mult, ALU.mult, accum_out=s2col)

        with h.sbt("atf", [128, 8, TOK], BF16) as atf, \
                h.sbt("znf", [128, 8, TOK], BF16) as znf:
            if sel is None:
                h.gdma(atf.ap(), at_d.rearrange("(lc p) t -> p lc t", p=128))
                h.flush()
            else:
                with h.sbt("sel_sb", [128, 2], F32) as sel_sb, \
                        h.sbt("atA", [128, TOK], BF16) as atA, \
                        h.sbt("atB", [128, TOK], BF16) as atB, \
                        h.sbt("atT", [128, TOK], BF16) as atT:
                    h.dma(sel_sb.ap(), sel)
                    agv = at_d.rearrange("(lc p) t -> p lc t", p=128)
                    for lc in range(8):
                        h.gdma(atA.ap(), agv[:, lc, 0:TOK])
                        h.gdma(atB.ap(), agv[:, lc, TOK:2 * TOK])
                        h.flush()
                        ins = nc.vector.tensor_scalar(atT.ap(), atA.ap(),
                                                      sel_sb.ap()[:, 0:1], None, ALU.mult)
                        h.vedge(ins)
                        nc.vector.scalar_tensor_tensor(atf.ap()[:, lc, :], atB.ap(),
                                                       sel_sb.ap()[:, 1:2], atT.ap(),
                                                       ALU.mult, ALU.add)
                        h.flush()
            with h.pst("pz0", [128, C], F32) as pz0, h.pst("pz1", [128, C], F32) as pz1:
                pzs = [pz0, pz1]
                for i in range(NCH + 1):
                    if i < NCH:
                        pz = pzs[i % 2]
                        for half in range(2):
                            sl = slice(half * 512, (half + 1) * 512)
                            for lc in range(8):
                                nc.tensor.matmul(pz.ap()[:, sl],
                                                 atf.ap()[:, lc, i * 128:(i + 1) * 128],
                                                 wo.ap()[:, lc, sl],
                                                 start=(lc == 0), stop=(lc == 7))
                    if i > 0:
                        zstats(zA.ap()[:, i - 1, :], pzs[(i - 1) % 2], None,
                               s2t.ap()[:, i - 1:i], with_s1=False)
                    h.bar()
            l2_batch(s2t.ap(), r1t.ap(), NCH)
            zn_tm = atf.ap().rearrange("p qc (a b) -> p (qc a) b", b=C)
            for cc in range(NCH):
                nc.vector.tensor_scalar(zn_tm[:, cc, :], zA.ap()[:, cc, :],
                                        r1t.ap()[:, cc:cc + 1], None, ALU.mult)
            h.bar()
            with h.pst("ps_ta", [128, 8, 128], BF16) as ps_ta, \
                    h.pst("ps_tb", [128, 8, 128], BF16) as ps_tb:
                h.pipe_transpose(
                    [([zn_tm[:, cc, dc * 128:(dc + 1) * 128] for dc in range(8)],
                      znf.ap()[:, :, cc * 128:(cc + 1) * 128]) for cc in range(NCH)],
                    ps_ta, ps_tb)
            with h.pst("pw0", [128, C], F32) as pw0, h.pst("pw1", [128, C], F32) as pw1:
                pws = [pw0, pw1]
                for i in range(NCH + 1):
                    if i < NCH:
                        pw = pws[i % 2]
                        for half in range(2):
                            sl = slice(half * 512, (half + 1) * 512)
                            for dc in range(8):
                                nc.tensor.matmul(pw.ap()[:, sl],
                                                 znf.ap()[:, dc, i * 128:(i + 1) * 128],
                                                 wff.ap()[:, dc, sl],
                                                 start=(dc == 0), stop=(dc == 7))
                    if i > 0:
                        zstats(zB.ap()[:, i - 1, :], pws[(i - 1) % 2], None,
                               s2t.ap()[:, i - 1:i], with_s1=False)
                    h.bar()
            l2_batch(s2t.ap(), r2t.ap(), NCH)

        gb = [st.enter_context(h.sbt(f"gb{j}", [128, C], BF16)) for j in range(2)]
        with h.sbt("z3", [128, 8, C], BF16) as z3, \
                h.sbt("x4o", [128, 8, C], F32) as x4o, \
                h.pst("pr0", [128, C], F32) as pr0, h.pst("pr1", [128, C], F32) as pr1:
            prs = [pr0, pr1]
            for th in range(2):
                ccs = list(range(th * 8, (th + 1) * 8))
                for i in range(8 + 2):
                    if i < 8:
                        cc = ccs[i]
                        nc.scalar.activation(gb[i % 2].ap(), zB.ap()[:, cc, :],
                                             AF.Gelu, scale=r2t.ap()[:, cc:cc + 1])
                    if 1 <= i <= 8:
                        cc = ccs[i - 1]
                        pr = prs[(i - 1) % 2]
                        for half in range(2):
                            sl = slice(half * 512, (half + 1) * 512)
                            nc.tensor.matmul(pr.ap()[:, sl], h.ident.ap(),
                                             x3_tm.ap()[:, cc, sl],
                                             start=True, stop=False)
                            nc.tensor.matmul(pr.ap()[:, sl], h.ident.ap(),
                                             gb[(i - 1) % 2].ap()[:, sl],
                                             start=False, stop=True)
                    if 2 <= i <= 9:
                        cc = ccs[i - 2]
                        zstats(z3.ap()[:, i - 2, :], prs[(i - 2) % 2],
                               s1t.ap()[:, cc:cc + 1], s2t.ap()[:, cc:cc + 1],
                               with_s1=True)
                    h.bar()
                h.ln_batch_stats(s1t.ap()[:, th * 8:(th + 1) * 8],
                                 s2t.ap()[:, th * 8:(th + 1) * 8],
                                 sc1t.ap()[:, th * 8:(th + 1) * 8],
                                 sc2t.ap()[:, th * 8:(th + 1) * 8], 8, C,
                                 h.cc("ln_eps"))
                for i, cc in enumerate(ccs):
                    nc.vector.tensor_scalar(x4o.ap()[:, i, :], z3.ap()[:, i, :],
                                            sc1t.ap()[:, cc:cc + 1],
                                            sc2t.ap()[:, cc:cc + 1],
                                            ALU.subtract, ALU.mult)
                h.bar()
                h.dma(o_x4.rearrange("(cc p) f -> p cc f", p=128)
                      [:, th * 8:(th + 1) * 8, :], x4o.ap())
                h.flush()


DBG_FUSED = False


def build_fused():
    nc = _nc()
    ext_i = lambda n, s, d: nc.dram_tensor(n, s, d, kind="ExternalInput").ap()
    intl = lambda n, s, d: nc.dram_tensor(n, s, d, kind="Internal").ap()

    xg = ext_i("xg", [TOK, C], F32)
    wqkvT0 = ext_i("wqkvT0", [C, 3 * C], F32)
    wprojT0 = ext_i("wprojT0", [C, C], F32)
    bproj0 = ext_i("bproj0", [C], F32)
    wgT = ext_i("wgT", [C, FFN_H], I32)
    wuT = ext_i("wuT", [C, FFN_H], I32)
    wdT = ext_i("wdT", [FFN_H, C], I32)
    wqkvT1 = ext_i("wqkvT1", [C, 3 * C], F32)
    lnw = ext_i("lnw", [C], F32)
    lnb = ext_i("lnb", [C], F32)
    woutT1 = ext_i("woutT1", [C, C], F32)
    cost = ext_i("cost", [NCH_IN * 128, 32], F32)
    ssint = ext_i("ssint", [NCH_IN * 128, 32], F32)
    mown = ext_i("mown", [128, 160], F32)
    mhalo = ext_i("mhalo", [128, 160], F32)
    wq1 = ext_i("wq1", [C, 512], F32)
    wk1 = ext_i("wk1", [C, 512], F32)
    wv1 = ext_i("wv1", [C, 512], F32)
    wo1 = ext_i("wo1", [C, C], F32)
    wff1 = ext_i("wff1", [C, C], F32)
    o_x4 = nc.dram_tensor("x4", [TOK, C], F32, kind="ExternalOutput").ap()

    mk1 = (lambda n, s, d: nc.dram_tensor(n, s, d, kind="ExternalOutput").ap()) \
        if DBG_FUSED else intl
    qf_d = mk1("i_qf", [C, TOK], BF16)
    kvp_in = intl("i_kvp_in", [128, 8 * 65], F32)
    kvp_out = intl("i_kvp_out", [128, 8 * 65], F32)
    x1_d = mk1("i_x1", [TOK, C], BF16)
    x2_d = mk1("i_x2", [TOK, C], BF16)
    halo_in = intl("i_halo_in", [128, C], BF16)
    halo_out = intl("i_halo_out", [256, C], BF16)
    x3_d = intl("i_x3", [TOK, C], BF16)
    x3f_d = intl("i_x3f", [T, C], BF16)
    attn_d2 = intl("i_attn", [512, T], BF16)
    attn_ag = intl("i_attn_ag", [2 * 512, T], BF16)
    sel = ext_i("sel", [128, 2], F32)

    with ExitStack() as gst:
        h = H(nc, gst)
        h.mk_ident()
        h.mk_consts(ALL_CONSTS)

        h.prefix = "A_"
        emit_A(nc, h, xg, wqkvT0, qf_d, None, None, kvp_in, kvp_4d=True)
        h.collective("AllReduce", ALU.add, [kvp_in.opt()], [kvp_out.opt()])
        h.prefix = "B_"
        emit_B(nc, h, qf_d, None, None, kvp_out, wprojT0, bproj0, xg, x1_d)
        h.prefix = "C_"
        emit_C(nc, h, x1_d, wgT, wuT, wdT, x2_d, halo_in)
        h.collective("AllGather", ALU.bypass, [halo_in.opt()], [halo_out.opt()])
        h.prefix = "D_"
        emit_D(nc, h, None, x2_d, halo_out, wqkvT1, lnw, lnb, woutT1, cost,
               ssint, mown, mhalo, x3_d)
        h.collective("AllGather", ALU.bypass, [x3_d.opt()], [x3f_d.opt()])
        h.prefix = "E_"
        emit_E(nc, h, x3f_d, wq1, wk1, wv1, attn_d2, a2a_mode=False)
        h.collective("AllGather", ALU.bypass, [attn_d2.opt()], [attn_ag.opt()])
        h.prefix = "F_"
        emit_F(nc, h, attn_ag, x3_d, wo1, wff1, o_x4, sel=sel)
        if DBG_FUSED:
            h.prefix = "Z_"
            d_x3 = nc.dram_tensor("d_x3", [TOK, C], BF16, kind="ExternalOutput").ap()
            d_x3f = nc.dram_tensor("d_x3f", [T, C], BF16, kind="ExternalOutput").ap()
            d_at = nc.dram_tensor("d_at", [2 * 512, T], BF16, kind="ExternalOutput").ap()
            d_kv = nc.dram_tensor("d_kv", [128, 8 * 65], F32, kind="ExternalOutput").ap()
            d_ho = nc.dram_tensor("d_ho", [256, C], BF16, kind="ExternalOutput").ap()
            with h.sbt("dump", [128, 8192], BF16) as dump:
                def dcopy(dst, srcap, rows, cols, dt=BF16):
                    nper = 8192 if dt == BF16 else 4096
                    for r0 in range(0, rows, 128):
                        rr = min(128, rows - r0)
                        for c0 in range(0, cols, nper):
                            ccw = min(nper, cols - c0)
                            sb = dump.ap()[0:rr, 0:ccw] if dt == BF16 else \
                                dump.ap().bitcast(F32)[0:rr, 0:ccw]
                            h.dma(sb, srcap[r0:r0 + rr, c0:c0 + ccw])
                            h.flush()
                            h.dma(dst[r0:r0 + rr, c0:c0 + ccw], sb)
                            h.flush()
                dcopy(d_x3, x3_d, TOK, C)
                dcopy(d_x3f, x3f_d, T, C)
                dcopy(d_at, attn_ag, 1024, T)
                dcopy(d_kv, kvp_out, 128, 520, F32)
                d_kvin = nc.dram_tensor("d_kvin", [128, 8 * 65], F32,
                                        kind="ExternalOutput").ap()
                dcopy(d_kvin, kvp_in, 128, 520, F32)
                dcopy(d_ho, halo_out, 256, C)
    return nc

def _prog(name):
    if name not in _PROG:
        _PROG[name] = {"A": build_A, "B": build_B, "C": build_C,
                       "D": build_D, "E": build_E, "F": build_F,
                       "FUSED": build_fused}[name]()
    return _PROG[name]


def _run(name, in_maps):
    res = run_bass_kernel_spmd(_prog(name), in_maps, core_ids=list(range(8)))
    return res.results


def _rope_tables():
    inv = (10000.0 ** (-(np.arange(0, DH, 2, dtype=np.float64) / DH))).astype(np.float64)
    tpos = np.arange(-128, T, dtype=np.float64)
    fr = tpos[:, None] * inv[None, :]
    fr = np.concatenate([fr, fr], axis=1)
    cos = np.cos(fr[:, 0:32]).astype(np.float32)
    sin = np.sin(fr[:, 0:32]).astype(np.float32)
    return cos, sin


def _kvp_pair(kvp):
    kvp = np.asarray(kvp, np.float32).reshape(64, HEADS, 65)
    out = np.zeros((128, 8, 65), np.float32)
    for hd in range(HEADS):
        par, pr = hd % 2, hd // 2
        out[par * 64:(par + 1) * 64, pr, :] = kvp[:, hd, :]
    return np.ascontiguousarray(out.reshape(128, 8 * 65))


def _band_masks():
    i = np.arange(128)[:, None]
    j = np.arange(160)[None, :]
    m_own = ((j - i >= 0) & (j - i < 32)).astype(np.float32)
    jj = j + 0
    m_halo = (((jj - 128) - i + 128 >= 0) & ((jj - 128) - i + 128 < 32) & (jj >= 128)).astype(np.float32)
    return m_own, m_halo


def kernel(**inputs):
    x = np.ascontiguousarray(np.asarray(inputs["L0_attn_x"], np.float32))
    cos, sin = _rope_tables()
    m_own, m_halo = _band_masks()
    zm = np.zeros_like(m_halo)
    com = {
        "wqkvT0": np.ascontiguousarray(np.asarray(inputs["L0_attn_Wqkv"], np.float32).T),
        "wprojT0": np.ascontiguousarray(np.asarray(inputs["L0_attn_Wproj"], np.float32).T),
        "bproj0": np.ascontiguousarray(np.asarray(inputs["L0_attn_bproj"], np.float32)),
        "wgT": np.ascontiguousarray(np.asarray(inputs["L0_ffn_w_gate"], np.int32).T),
        "wuT": np.ascontiguousarray(np.asarray(inputs["L0_ffn_w_up"], np.int32).T),
        "wdT": np.ascontiguousarray(np.asarray(inputs["L0_ffn_w_down"], np.int32).T),
        "wqkvT1": np.ascontiguousarray(np.asarray(inputs["L1_attn_w_qkv"], np.float32).T),
        "lnw": np.ascontiguousarray(np.asarray(inputs["L1_attn_ln_w"], np.float32)),
        "lnb": np.ascontiguousarray(np.asarray(inputs["L1_attn_ln_b"], np.float32)),
        "woutT1": np.ascontiguousarray(np.asarray(inputs["L1_attn_w_out"], np.float32).T),
        "mown": m_own,
        "wo1": np.ascontiguousarray(np.asarray(inputs["L1_ffn_Wo"], np.float32)),
        "wff1": np.ascontiguousarray(np.asarray(inputs["L1_ffn_Wff"], np.float32)),
    }
    wq1 = np.asarray(inputs["L1_ffn_Wq"], np.float32)
    wk1 = np.asarray(inputs["L1_ffn_Wk"], np.float32)
    wv1 = np.asarray(inputs["L1_ffn_Wv"], np.float32)
    ins = []
    for c in range(8):
        b, r0 = c // 2, (c % 2) * TOK
        hsel = slice((c % 2) * 512, (c % 2) * 512 + 512)
        tsl = slice(r0, r0 + TOK + 128)
        selv = np.zeros((128, 2), np.float32)
        selv[:, c % 2] = 1.0
        m = dict(com)
        m.update({
            "xg": np.ascontiguousarray(x[b, r0:r0 + TOK]),
            "cost": np.ascontiguousarray(cos[tsl]),
            "ssint": np.ascontiguousarray(sin[tsl]),
            "mhalo": (m_halo if r0 > 0 else zm),
            "wq1": np.ascontiguousarray(wq1[:, hsel]),
            "wk1": np.ascontiguousarray(wk1[:, hsel]),
            "wv1": np.ascontiguousarray(wv1[:, hsel]),
            "sel": selv,
        })
        ins.append(m)
    r = _run("FUSED", ins)
    global _LAST_RESULTS
    _LAST_RESULTS = r
    out = np.stack([np.concatenate([r[2 * b]["x4"], r[2 * b + 1]["x4"]], axis=0)
                    for b in range(B)])
    return out.astype(np.float32)


def kernel_multi(L0_attn_x, L0_attn_Wqkv, L0_attn_Wproj, L0_attn_bproj,
                 L0_ffn_w_gate, L0_ffn_w_up, L0_ffn_w_down,
                 L1_attn_ln_w, L1_attn_ln_b, L1_attn_w_qkv, L1_attn_w_out,
                 L1_ffn_Wq, L1_ffn_Wk, L1_ffn_Wv, L1_ffn_Wo, L1_ffn_Wff):
    x = np.ascontiguousarray(np.asarray(L0_attn_x, dtype=np.float32))
    wqkvT0 = np.ascontiguousarray(np.asarray(L0_attn_Wqkv, np.float32).T)
    wprojT0 = np.ascontiguousarray(np.asarray(L0_attn_Wproj, np.float32).T)
    bproj0 = np.ascontiguousarray(np.asarray(L0_attn_bproj, np.float32))
    wgT = np.ascontiguousarray(np.asarray(L0_ffn_w_gate, np.int32).T)
    wuT = np.ascontiguousarray(np.asarray(L0_ffn_w_up, np.int32).T)
    wdT = np.ascontiguousarray(np.asarray(L0_ffn_w_down, np.int32).T)
    lnw = np.ascontiguousarray(np.asarray(L1_attn_ln_w, np.float32))
    lnb = np.ascontiguousarray(np.asarray(L1_attn_ln_b, np.float32))
    wqkvT1 = np.ascontiguousarray(np.asarray(L1_attn_w_qkv, np.float32).T)
    woutT1 = np.ascontiguousarray(np.asarray(L1_attn_w_out, np.float32).T)
    wq1 = np.ascontiguousarray(np.asarray(L1_ffn_Wq, np.float32))
    wk1 = np.ascontiguousarray(np.asarray(L1_ffn_Wk, np.float32))
    wv1 = np.ascontiguousarray(np.asarray(L1_ffn_Wv, np.float32))
    wo1 = np.ascontiguousarray(np.asarray(L1_ffn_Wo, np.float32))
    wff1 = np.ascontiguousarray(np.asarray(L1_ffn_Wff, np.float32))

    def slc(c):
        b, hf = c // 2, c % 2
        return b, hf * TOK

    ins = []
    for c in range(8):
        b, r0 = slc(c)
        ins.append({"xg": np.ascontiguousarray(x[b, r0:r0 + TOK]), "wqkvT": wqkvT0})
    rA = _run("A", ins)

    ins = []
    for c in range(8):
        b, r0 = slc(c)
        sib = c ^ 1
        ins.append({"qf": rA[c]["qf"], "kvp0": _kvp_pair(rA[c]["kvp"]),
                    "kvp1": _kvp_pair(rA[sib]["kvp"]),
                    "wprojT": wprojT0, "bproj": bproj0,
                    "xg": np.ascontiguousarray(x[b, r0:r0 + TOK])})
    rB = _run("B", ins)
    x1 = np.stack([np.concatenate([rB[2 * b]["x1"], rB[2 * b + 1]["x1"]], axis=0)
                   for b in range(B)])

    ins = []
    for c in range(8):
        b, r0 = slc(c)
        ins.append({"x1": np.ascontiguousarray(x1[b, r0:r0 + TOK]),
                    "wgT": wgT, "wuT": wuT, "wdT": wdT})
    rC = _run("C", ins)
    x2 = np.stack([np.concatenate([rC[2 * b]["x2"], rC[2 * b + 1]["x2"]], axis=0)
                   for b in range(B)])

    cos, ssin = _rope_tables()
    m_own, m_halo = _band_masks()
    zeros_mask = np.zeros_like(m_halo)
    ins = []
    for c in range(8):
        b, r0 = slc(c)
        if r0 == 0:
            x2in = np.concatenate([np.zeros((128, C), x2.dtype), x2[b, 0:TOK]], axis=0)
        else:
            x2in = np.ascontiguousarray(x2[b, r0 - 128:r0 + TOK])
        tsl = slice(r0 - 128 + 128, r0 + TOK + 128)
        ins.append({"x2in": x2in, "wqkvT": wqkvT1, "lnw": lnw, "lnb": lnb,
                    "woutT": woutT1,
                    "cost": np.ascontiguousarray(cos[tsl]),
                    "ssint": np.ascontiguousarray(ssin[tsl]),
                    "mown": m_own,
                    "mhalo": (m_halo if r0 > 0 else zeros_mask)})
    rD = _run("D", ins)
    x3 = np.stack([np.concatenate([rD[2 * b]["x3"], rD[2 * b + 1]["x3"]], axis=0)
                   for b in range(B)])

    ins = []
    for c in range(8):
        b, r0 = slc(c)
        hsel = slice((c % 2) * 512, (c % 2) * 512 + 512)
        ins.append({"x3f": np.ascontiguousarray(x3[b]),
                    "wq": np.ascontiguousarray(wq1[:, hsel]),
                    "wk": np.ascontiguousarray(wk1[:, hsel]),
                    "wv": np.ascontiguousarray(wv1[:, hsel])})
    rE = _run("E", ins)
    attn = np.stack([np.concatenate([rE[2 * b]["attn"], rE[2 * b + 1]["attn"]], axis=0)
                     for b in range(B)])

    ins = []
    for c in range(8):
        b, r0 = slc(c)
        ins.append({"attn_own": np.ascontiguousarray(attn[b][:, r0:r0 + TOK]),
                    "x3own": np.ascontiguousarray(x3[b, r0:r0 + TOK]),
                    "wo": wo1, "wff": wff1})
    rF = _run("F", ins)
    out = np.stack([np.concatenate([rF[2 * b]["x4"], rF[2 * b + 1]["x4"]], axis=0)
                    for b in range(B)])
    return out.astype(np.float32)


# version 12 (final)
# speedup vs baseline: 1.0152x; 1.0152x over previous
import os
if "AXON_NCC_FLAGS" in os.environ:
    os.environ["AXON_NCC_FLAGS"] = os.environ["AXON_NCC_FLAGS"].replace(
        "--enable-ldw-opt=false", "--enable-ldw-opt=true")

import numpy as np
import ml_dtypes
from contextlib import ExitStack

import concourse.bass as bass
import concourse.mybir as mybir
from concourse.bass_utils import run_bass_kernel_spmd

F32 = mybir.dt.float32
BF16 = mybir.dt.bfloat16
I32 = mybir.dt.int32
AF = mybir.ActivationFunctionType
ALU = mybir.AluOpType
AX = mybir.AxisListType

BF = ml_dtypes.bfloat16

B, T, C = 4, 4096, 1024
TOK = 2048
NCH = 16
HEADS, DH = 16, 64
FFN_H = 3072
QS = 1.0 / 127.0
LN_EPS = 1e-5
ATT_EPS = 1e-6
L2_EPS = 1e-12
EXPC = 16.0

_PROG = {}
_LAST_RESULTS = None


def _ap_bcast(a, times):
    dims = [list(d) for d in a.ap]
    dims = dims[:-1] + [[0, times]] + [dims[-1]]
    return bass.AP(a.tensor, a.offset, dims)


def _ap_bcast_last(a, times):
    dims = [list(d) for d in a.ap] + [[0, times]]
    return bass.AP(a.tensor, a.offset, dims)


class H:

    def __init__(self, nc, st):
        self.nc = nc
        self.st = st
        self.prefix = ""
        self.dsem = st.enter_context(nc.semaphore("dsem"))
        self.gsem = st.enter_context(nc.semaphore("gsem"))
        self.ccsem = st.enter_context(nc.semaphore("ccsem"))
        self.dcnt = 0
        self.gcnt = 0
        self.nccs = 0
        self._sems = {}

    def sbt(self, name, shape, dt):
        return self.nc.sbuf_tensor(self.prefix + name, shape, dt)

    def pst(self, name, shape, dt=F32):
        return self.nc.psum_tensor(self.prefix + name, shape, dt)

    def sem(self, name):
        key = self.prefix + name
        if key not in self._sems:
            self._sems[key] = self.st.enter_context(self.nc.semaphore(key))
        return self._sems[key]

    def collective_start(self, kind, op, ins, outs):
        self.nc.gpsimd.collective_compute(
            kind, op, replica_groups=[[0, 1], [2, 3], [4, 5], [6, 7]],
            ins=ins, outs=outs).then_inc(self.ccsem)
        self.nccs += 1

    def collective_await(self):
        self.nc.gpsimd.wait_ge(self.ccsem, self.nccs)
        self.bar()

    def collective(self, kind, op, ins, outs):
        self.collective_start(kind, op, ins, outs)
        self.collective_await()

    def bar(self):
        self.nc.all_engine_barrier()

    def dma(self, out, in_):
        self.nc.sync.dma_start(out=out, in_=in_).then_inc(self.dsem, 16)
        self.dcnt += 16

    def gdma(self, out, in_):
        self.nc.gpsimd.dma_start(out=out, in_=in_).then_inc(self.gsem, 16)
        self.gcnt += 16

    def dwait(self):
        if self.dcnt:
            self.nc.sync.wait_ge(self.dsem, self.dcnt)
        if self.gcnt:
            self.nc.gpsimd.wait_ge(self.gsem, self.gcnt)

    def flush(self):
        self.dwait()
        self.bar()

    def mk_ident(self):
        nc, st = self.nc, self.st
        self.ident = st.enter_context(nc.sbuf_tensor("ident", [128, 128], BF16))
        self.ident2 = st.enter_context(nc.sbuf_tensor("ident2", [128, 128], BF16))
        self.ones_row = st.enter_context(nc.sbuf_tensor("ones_row", [1, 128], BF16))
        self.ones_col = st.enter_context(nc.sbuf_tensor("ones_col", [128, 1], BF16))
        with nc.sbuf_tensor("iota_a", [128, 128], I32) as ia, \
                nc.sbuf_tensor("iota_b", [128, 128], I32) as ib:
            nc.gpsimd.iota(ia.ap(), [[1, 128]], channel_multiplier=0)
            nc.gpsimd.iota(ib.ap(), [[0, 128]], channel_multiplier=1)
            nc.vector.memset(self.ones_row.ap(), 1.0)
            nc.vector.memset(self.ones_col.ap(), 1.0)
            self.bar()
            nc.vector.tensor_tensor(self.ident.ap(), ia.ap(), ib.ap(), ALU.is_equal)
            self.bar()
            nc.vector.tensor_scalar_mul(self.ident2.ap(), self.ident.ap(), 2.0)
            self.bar()

    def mk_consts(self, spec):
        nc = self.nc
        n = len(spec)
        self.ctile = self.st.enter_context(nc.sbuf_tensor("consts", [128, n], F32))
        self.cmap = {}
        for i, (k, v) in enumerate(spec.items()):
            nc.vector.memset(self.ctile.ap()[:, i:i + 1], float(v))
            self.cmap[k] = i
        self.bar()

    def cc(self, name):
        i = self.cmap[name]
        return self.ctile.ap()[:, i:i + 1]


    def transpose_tile(self, ps_bf, src_ap, dst_ap):
        nc = self.nc
        nc.tensor.transpose(ps_bf.ap(), src_ap, self.ident.ap())
        self.bar()
        nc.scalar.copy(dst_ap, ps_bf.ap())
        self.bar()

    def transpose_group(self, ps_g, srcs, dst_ap):
        nc = self.nc
        n = len(srcs)
        for j, s in enumerate(srcs):
            nc.tensor.transpose(ps_g.ap()[:, j, :], s, self.ident.ap())
        self.bar()
        nc.scalar.copy(dst_ap, ps_g.ap()[:, 0:n, :])
        self.bar()

    def vedge(self, ins):
        s = self.sem("vchain")
        key = self.prefix + "vchain_cnt"
        n = self._sems.get(key, 0) + 1
        self._sems[key] = n
        ins.then_inc(s, 1)
        self.nc.vector.wait_ge(s, n)

    def pipe(self, n, mms, stages, nbuf, name="d"):
        nc = self.nc
        eng = {"tensor": nc.tensor, "scalar": nc.scalar, "vector": nc.vector,
               "gpsimd": nc.gpsimd}
        ns = len(stages)
        sems = [self.sem(f"pp{name}_s{k}") for k in range(ns + 1)]
        key = self.prefix + "pipe_" + name
        base = self._sems.get(key, 0)
        self._sems[key] = base + n
        for i in range(n + ns):
            for k, (ename, fn) in enumerate(stages):
                u = i - 1 - k
                if 0 <= u < n:
                    e = eng[ename]
                    e.wait_ge(sems[k], base + u + 1)
                    ins = fn(u)
                    if ins is None:
                        ins = e.nop()
                    ins.then_inc(sems[k + 1], 1)
            if i < n:
                if i >= nbuf:
                    nc.tensor.wait_ge(sems[ns], base + i - nbuf + 1)
                ins = mms(i)
                ins.then_inc(sems[0], 1)
        self.bar()

    def pipe_transpose(self, groups, ps_a, ps_b):
        nc = self.nc
        bufs = [ps_a, ps_b]
        n = len(groups)

        def mms(i):
            srcs = groups[i][0]
            for j, s in enumerate(srcs):
                ins = nc.tensor.transpose(bufs[i % 2].ap()[:, j, :], s,
                                          self.ident.ap())
            return ins

        def cp(u):
            srcs_p, dst_p = groups[u]
            return nc.scalar.copy(dst_p, bufs[u % 2].ap()[:, 0:len(srcs_p), :])

        self.pipe(n, mms, [("scalar", cp)], 2, name="tr")

    def ln_batch_stats(self, s1t, s2t, sc1t, sc2t, n, cw, eps_cc):
        nc = self.nc
        nc.vector.tensor_scalar_mul(sc1t[:, 0:n], s1t[:, 0:n], 1.0 / cw)
        self.bar()
        nc.vector.tensor_mul(s1t[:, 0:n], sc1t[:, 0:n], sc1t[:, 0:n])
        self.bar()
        nc.vector.scalar_tensor_tensor(s2t[:, 0:n], s2t[:, 0:n], 1.0 / cw,
                                       s1t[:, 0:n], ALU.mult, ALU.subtract)
        self.bar()
        nc.scalar.activation(s2t[:, 0:n], s2t[:, 0:n], AF.Sqrt, bias=eps_cc)
        self.bar()
        nc.vector.reciprocal(sc2t[:, 0:n], s2t[:, 0:n])
        self.bar()

    def ln_psum(self, ps_ap, out_ap, s1, s2, junk_a, junk_b, eps_cc, sc1, sc2):
        nc = self.nc
        cw = ps_ap.shape[-1]
        nc.vector.tensor_scalar(junk_a, ps_ap, 1.0, 0.0, ALU.mult, ALU.add,
                                accum_out=s1)
        self.bar()
        nc.scalar.activation(junk_b, ps_ap, AF.Square, accum_out=s2)
        self.bar()
        nc.vector.tensor_scalar_mul(sc1, s1, 1.0 / cw)
        self.bar()
        nc.vector.tensor_mul(s1, sc1, sc1)
        self.bar()
        nc.vector.scalar_tensor_tensor(s2, s2, 1.0 / cw, s1, ALU.mult, ALU.subtract)
        self.bar()
        nc.scalar.activation(s2, s2, AF.Sqrt, bias=eps_cc)
        self.bar()
        nc.vector.reciprocal(sc2, s2)
        self.bar()
        nc.vector.tensor_scalar(out_ap, ps_ap, sc1, sc2, ALU.subtract, ALU.mult)
        self.bar()


def _nc():
    return bass.Bass("TRN2", target_bir_lowering=False, debug=False)


ALL_CONSTS = {"neg1": -1.0, "ln_eps": LN_EPS, "qs": QS, "s8": 0.125,
              "negC": -EXPC, "s32": 1.0 / 32.0}


def build_A():
    nc = _nc()
    xg = nc.dram_tensor("xg", [TOK, C], F32, kind="ExternalInput").ap()
    wqkvT = nc.dram_tensor("wqkvT", [C, 3 * C], F32, kind="ExternalInput").ap()
    o_qf = nc.dram_tensor("qf", [C, TOK], BF16, kind="ExternalOutput").ap()
    o_ktm = nc.dram_tensor("ktm", [TOK, C], BF16, kind="ExternalOutput").ap()
    o_vtm = nc.dram_tensor("vtm", [TOK, C], BF16, kind="ExternalOutput").ap()
    o_kvp = nc.dram_tensor("kvp", [64, HEADS * 65], F32, kind="ExternalOutput").ap()
    with ExitStack() as gst:
        h = H(nc, gst)
        h.mk_ident()
        h.mk_consts(ALL_CONSTS)
        emit_A(nc, h, xg, wqkvT, o_qf, o_ktm, o_vtm, o_kvp)
    return nc


def emit_A(nc, h, xg, wqkvT, o_qf, o_ktm, o_vtm, o_kvp, kvp_4d=False):
    with ExitStack() as st:
        w_sb = st.enter_context(h.sbt("w_sb", [128, 8, 3 * C], BF16))
        x_fm = st.enter_context(h.sbt("x_fm", [128, 8, TOK], BF16))
        t1 = [st.enter_context(h.sbt(f"t1_{j}", [128, 512], F32)) for j in range(3)]
        t2 = [st.enter_context(h.sbt(f"t2_{j}", [128, 512], F32)) for j in range(3)]

        with h.sbt("x_tm", [128, NCH, C], BF16) as x_tm, \
                h.pst("ps_ta", [128, 8, 128], BF16) as ps_ta, \
                h.pst("ps_tb", [128, 8, 128], BF16) as ps_tb:
            h.gdma(x_tm.ap(), xg.rearrange("(cc p) f -> p cc f", p=128))
            h.gdma(w_sb.ap(), wqkvT.rearrange("(dc p) e -> p dc e", p=128))
            h.flush()
            h.pipe_transpose(
                [([x_tm.ap()[:, cc, dc * 128:(dc + 1) * 128] for dc in range(8)],
                  x_fm.ap()[:, :, cc * 128:(cc + 1) * 128]) for cc in range(NCH)],
                ps_ta, ps_tb)

        qf = st.enter_context(h.sbt("qf_sb", [128, 8, TOK], BF16))
        ktm = st.enter_context(h.sbt("ktm_sb", [128, NCH, C], BF16))
        vex = st.enter_context(h.sbt("vex_sb", [128, NCH, HEADS * 65], BF16))
        nc.vector.memset(vex.ap().rearrange("p cc (hd f) -> p cc hd f", f=65)
                         [:, :, :, 64:65], 1.0)

        units = []
        for qc in range(8):
            for ts in range(4):
                units.append(("q",
                              [w_sb.ap()[:, dc, qc * 128:(qc + 1) * 128] for dc in range(8)],
                              [x_fm.ap()[:, dc, ts * 512:(ts + 1) * 512] for dc in range(8)],
                              qf.ap()[:, qc, ts * 512:(ts + 1) * 512]))
        for cc in range(NCH):
            for half in range(2):
                units.append(("k",
                              [x_fm.ap()[:, dc, cc * 128:(cc + 1) * 128] for dc in range(8)],
                              [w_sb.ap()[:, dc, C + half * 512:C + (half + 1) * 512] for dc in range(8)],
                              ktm.ap()[:, cc, half * 512:(half + 1) * 512]))
            for half in range(2):
                units.append(("v",
                              [x_fm.ap()[:, dc, cc * 128:(cc + 1) * 128] for dc in range(8)],
                              [w_sb.ap()[:, dc, 2 * C + half * 512:2 * C + (half + 1) * 512] for dc in range(8)],
                              vex.ap().rearrange("p cc2 (hd f) -> p cc2 hd f", f=65)
                              [:, cc, half * 8:(half + 1) * 8, 0:64]))

        with h.pst("ps_m0", [128, 512], F32) as ps_m0, \
                h.pst("ps_m1", [128, 512], F32) as ps_m1, \
                h.pst("ps_m2", [128, 512], F32) as ps_m2:
            bufs = [ps_m0, ps_m1, ps_m2]

            def mms(i):
                _, ls, rs, _ = units[i]
                for dc in range(8):
                    ins = nc.tensor.matmul(bufs[i % 3].ap(), ls[dc], rs[dc],
                                           start=(dc == 0), stop=(dc == 7))
                return ins

            def st1(u):
                kind, _, _, dst = units[u]
                pb = bufs[u % 3]
                if kind == "v":
                    return nc.scalar.copy(dst, pb.ap().rearrange("p (hd f) -> p hd f", f=64))
                nc.scalar.activation(t1[u % 3].ap(), pb.ap(), AF.Relu,
                                     scale=h.cc("neg1"))
                return nc.scalar.activation(t2[u % 3].ap(), t1[u % 3].ap(),
                                            AF.Exp, scale=h.cc("neg1"))

            def st2(u):
                kind, _, _, dst = units[u]
                if kind == "v":
                    return None
                return nc.vector.scalar_tensor_tensor(dst, bufs[u % 3].ap(), 0.0,
                                                      t2[u % 3].ap(),
                                                      ALU.max, ALU.add)

            h.pipe(len(units), mms, [("scalar", st1), ("vector", st2)], 3, name="qkv")

        with h.pst("ps_kv", [64, HEADS, 128], F32) as ps_kv, \
                h.sbt("kvp_sb", [64, HEADS * 65], F32) as kvp_sb:
            for hd in range(HEADS):
                for cc in range(NCH):
                    nc.tensor.matmul(ps_kv.ap()[:, hd, 0:65],
                                     ktm.ap()[:, cc, hd * 64:(hd + 1) * 64],
                                     vex.ap().rearrange("p cc2 (h2 f) -> p cc2 h2 f", f=65)
                                     [:, cc, hd, :],
                                     start=(cc == 0), stop=(cc == NCH - 1))
            h.bar()
            nc.vector.tensor_copy(kvp_sb.ap().rearrange("p (hd f) -> p hd f", f=65),
                                  ps_kv.ap()[:, :, 0:65])
            h.bar()
            if kvp_4d:
                sv = kvp_sb.ap()
                src_kv = bass.AP(sv.tensor, sv.offset,
                                 [[HEADS * 65, 64], [130, 8], [65, 2], [1, 65]])
                dst_kv = bass.AP(o_kvp.tensor, o_kvp.offset,
                                 [[520, 64], [65, 8], [520 * 64, 2], [1, 65]])
                h.dma(dst_kv, src_kv)
            else:
                h.dma(o_kvp, kvp_sb.ap())

        h.dma(o_qf.rearrange("(qc p) t -> p qc t", p=128), qf.ap())
        if o_ktm is not None:
            h.dma(o_ktm.rearrange("(cc p) f -> p cc f", p=128), ktm.ap())
        if o_vtm is not None:
            for cc in range(NCH):
                h.dma(o_vtm.rearrange("(cc p) f -> p cc f", p=128)[:, cc, :]
                      .rearrange("p (hd f) -> p hd f", f=64),
                      vex.ap().rearrange("p cc2 (hd f) -> p cc2 hd f", f=65)
                      [:, cc, :, 0:64])
        h.flush()


def build_B():
    nc = _nc()
    qf_d = nc.dram_tensor("qf", [C, TOK], BF16, kind="ExternalInput").ap()
    kvp0 = nc.dram_tensor("kvp0", [128, 8 * 65], F32, kind="ExternalInput").ap()
    kvp1 = nc.dram_tensor("kvp1", [128, 8 * 65], F32, kind="ExternalInput").ap()
    wprojT = nc.dram_tensor("wprojT", [C, C], F32, kind="ExternalInput").ap()
    bproj = nc.dram_tensor("bproj", [C], F32, kind="ExternalInput").ap()
    xg = nc.dram_tensor("xg", [TOK, C], F32, kind="ExternalInput").ap()
    o_x1 = nc.dram_tensor("x1", [TOK, C], BF16, kind="ExternalOutput").ap()
    with ExitStack() as gst:
        h = H(nc, gst)
        h.mk_ident()
        h.mk_consts(ALL_CONSTS)
        emit_B(nc, h, qf_d, kvp0, kvp1, None, wprojT, bproj, xg, o_x1)
    return nc


def emit_B(nc, h, qf_d, kvp0, kvp1, kvp_red, wprojT, bproj, xg, o_x1,
           await_cb=None):
    with ExitStack() as st:
        qf = st.enter_context(h.sbt("qf_sb", [128, 8, TOK], BF16))
        x_tm = st.enter_context(h.sbt("x_tm", [128, NCH, C], BF16))
        kvE = st.enter_context(h.sbt("kvE", [128, 8, 65], BF16))
        wp = st.enter_context(h.sbt("wp", [128, 8, C], BF16))
        bp = st.enter_context(h.sbt("bp", [1, C], BF16))
        z_sb = st.enter_context(h.sbt("z_sb", [128, NCH, C], BF16))
        s1t = st.enter_context(h.sbt("s1t", [128, NCH], F32))
        s2t = st.enter_context(h.sbt("s2t", [128, NCH], F32))
        sc1t = st.enter_context(h.sbt("sc1t", [128, NCH], F32))
        sc2t = st.enter_context(h.sbt("sc2t", [128, NCH], F32))
        junk_a = st.enter_context(h.sbt("junk_a", [128, C], BF16))
        rec = [st.enter_context(h.sbt(f"rec{j}", [128, HEADS], F32)) for j in range(2)]
        radd = [st.enter_context(h.sbt(f"radd{j}", [128, HEADS], F32)) for j in range(2)]

        with h.sbt("kva", [128, 8 * 65], F32) as kva, \
                h.sbt("kvb", [128, 8 * 65], F32) as kvb:
            h.gdma(qf.ap(), qf_d.rearrange("(qc p) t -> p qc t", p=128))
            h.gdma(x_tm.ap(), xg.rearrange("(cc p) f -> p cc f", p=128))
            h.gdma(wp.ap(), wprojT.rearrange("(dc p) o -> p dc o", p=128))
            h.gdma(bp.ap(), bproj.rearrange("(a f) -> a f", a=1))
            if kvp_red is not None:
                if await_cb is not None:
                    await_cb()
                h.gdma(kvE.ap().rearrange("p pr f -> p (pr f)"), kvp_red)
                h.flush()
            else:
                h.dma(kva.ap(), kvp0)
                h.dma(kvb.ap(), kvp1)
                h.flush()
                nc.vector.tensor_add(kvE.ap().rearrange("p pr f -> p (pr f)"),
                                     kva.ap(), kvb.ap())
                h.flush()

        with h.sbt("attn_tm", [128, NCH, C], BF16) as attn_tm, \
                h.sbt("attn_fm", [128, 8, TOK], BF16) as attn_fm:
            with h.pst("poA0", [128, 8, 128], F32) as poA0, \
                    h.pst("poB0", [128, 8, 128], F32) as poB0, \
                    h.pst("poA1", [128, 8, 128], F32) as poA1, \
                    h.pst("poB1", [128, 8, 128], F32) as poB1:
                gens = [(poA0, poB0), (poA1, poB1)]

                def mms(i):
                    poA, poB = gens[i % 2]
                    for pr in range(8):
                        for par, po in ((0, poA), (1, poB)):
                            hp = par * 64
                            ins = nc.tensor.matmul(po.ap()[:, pr, 0:65],
                                                   qf.ap()[hp:hp + 64, pr,
                                                           i * 128:(i + 1) * 128],
                                                   kvE.ap()[hp:hp + 64, pr, :],
                                                   start=True, stop=True,
                                                   tile_position=(hp, 0))
                    return ins

                def epi(cc):
                    poA, poB = gens[cc % 2]
                    for par, po in ((0, poA), (1, poB)):
                        ins = nc.vector.tensor_scalar(
                            radd[cc % 2].ap()[:, par * 8:(par + 1) * 8],
                            po.ap()[:, :, 64:65].squeeze(2),
                            ATT_EPS, 0.0, ALU.add, ALU.add)
                    h.vedge(ins)
                    ins = nc.vector.reciprocal(rec[cc % 2].ap(), radd[cc % 2].ap())
                    h.vedge(ins)
                    for par, po in ((0, poA), (1, poB)):
                        dstv = bass.AP(attn_tm.ap().tensor,
                                       attn_tm.ap().offset + cc * C + par * 64,
                                       [list(attn_tm.ap().ap[0]), [128, 8], [1, 64]])
                        ins = nc.vector.tensor_tensor(
                            dstv, po.ap()[:, :, 0:64],
                            _ap_bcast_last(rec[cc % 2].ap()[:, par * 8:(par + 1) * 8], 64),
                            ALU.mult)
                    return ins

                h.pipe(NCH, mms, [("vector", epi)], 2, name="oe")

            with h.pst("ps_ta", [128, 8, 128], BF16) as ps_ta, \
                    h.pst("ps_tb", [128, 8, 128], BF16) as ps_tb:
                h.pipe_transpose(
                    [([attn_tm.ap()[:, cc, dc * 128:(dc + 1) * 128] for dc in range(8)],
                      attn_fm.ap()[:, :, cc * 128:(cc + 1) * 128]) for cc in range(NCH)],
                    ps_ta, ps_tb)

            with h.pst("pz0", [128, C], F32) as pz0, \
                    h.pst("pz1", [128, C], F32) as pz1:
                pzs = [pz0, pz1]

                def mms2(i):
                    pz = pzs[i % 2]
                    for half in range(2):
                        sl = slice(half * 512, (half + 1) * 512)
                        for dc in range(8):
                            nc.tensor.matmul(pz.ap()[:, sl],
                                             attn_fm.ap()[:, dc, i * 128:(i + 1) * 128],
                                             wp.ap()[:, dc, sl],
                                             start=(dc == 0), stop=False)
                        nc.tensor.matmul(pz.ap()[:, sl], h.ones_row.ap(),
                                         bp.ap()[:, sl], start=False, stop=False)
                        ins = nc.tensor.matmul(pz.ap()[:, sl], h.ident.ap(),
                                               x_tm.ap()[:, i, sl],
                                               start=False, stop=True)
                    return ins

                def epi2(cc):
                    pz = pzs[cc % 2]
                    ins = nc.vector.tensor_scalar(z_sb.ap()[:, cc, :], pz.ap(),
                                                  1.0, 0.0, ALU.mult, ALU.add,
                                                  accum_out=s1t.ap()[:, cc:cc + 1])
                    h.vedge(ins)
                    return nc.vector.scalar_tensor_tensor(
                        junk_a.ap(), z_sb.ap()[:, cc, :], 1.0, z_sb.ap()[:, cc, :],
                        ALU.mult, ALU.mult, accum_out=s2t.ap()[:, cc:cc + 1])

                h.pipe(NCH, mms2, [("vector", epi2)], 2, name="pj")
            h.ln_batch_stats(s1t.ap(), s2t.ap(), sc1t.ap(), sc2t.ap(), NCH, C,
                             h.cc("ln_eps"))
            for cc in range(NCH):
                nc.vector.tensor_scalar(attn_tm.ap()[:, cc, :], z_sb.ap()[:, cc, :],
                                        sc1t.ap()[:, cc:cc + 1], sc2t.ap()[:, cc:cc + 1],
                                        ALU.subtract, ALU.mult)
            h.bar()
            h.dma(o_x1.rearrange("(cc p) f -> p cc f", p=128), attn_tm.ap())
            h.flush()


def build_C():
    nc = _nc()
    x1_d = nc.dram_tensor("x1", [TOK, C], BF16, kind="ExternalInput").ap()
    wgT = nc.dram_tensor("wgT", [C, FFN_H], I32, kind="ExternalInput").ap()
    wuT = nc.dram_tensor("wuT", [C, FFN_H], I32, kind="ExternalInput").ap()
    wdT = nc.dram_tensor("wdT", [FFN_H, C], I32, kind="ExternalInput").ap()
    o_x2 = nc.dram_tensor("x2", [TOK, C], BF16, kind="ExternalOutput").ap()
    with ExitStack() as gst:
        h = H(nc, gst)
        h.mk_ident()
        h.mk_consts(ALL_CONSTS)
        emit_C(nc, h, x1_d, wgT, wuT, wdT, o_x2, None)
    return nc


def emit_C(nc, h, x1_d, wgT, wuT, wdT, o_x2, halo_in):
    with ExitStack() as st:
        x1_tm = st.enter_context(h.sbt("x1_tm", [128, NCH, C], BF16))
        wd = st.enter_context(h.sbt("wd", [128, 24, C], BF16))
        hsb = st.enter_context(h.sbt("hsb", [128, 24, 1024], BF16))
        s1t = st.enter_context(h.sbt("s1t", [128, NCH], F32))
        s2t = st.enter_context(h.sbt("s2t", [128, NCH], F32))
        sc1t = st.enter_context(h.sbt("sc1t", [128, NCH], F32))
        sc2t = st.enter_context(h.sbt("sc2t", [128, NCH], F32))
        junk_a = st.enter_context(h.sbt("junk_a", [128, C], BF16))
        sg = [st.enter_context(h.sbt(f"sg{j}", [128, 512], F32)) for j in range(3)]

        h.gdma(x1_tm.ap(), x1_d.rearrange("(cc p) f -> p cc f", p=128))
        h.gdma(wd.ap(), wdT.rearrange("(hc p) d -> p hc d", p=128))
        h.flush()

        with h.sbt("x1f", [128, 8, 1024], BF16) as x1f, \
                h.sbt("zsb", [128, 8, C], BF16) as zsb:
            for th in range(2):
                ccs = list(range(th * 8, (th + 1) * 8))
                with h.pst(f"ps_ta_{th}", [128, 8, 128], BF16) as ps_ta, \
                        h.pst(f"ps_tb_{th}", [128, 8, 128], BF16) as ps_tb:
                    h.pipe_transpose(
                        [([x1_tm.ap()[:, cc, dc * 128:(dc + 1) * 128] for dc in range(8)],
                          x1f.ap()[:, :, i * 128:(i + 1) * 128])
                         for i, cc in enumerate(ccs)],
                        ps_ta, ps_tb)
                for sec in range(3):
                    with h.sbt(f"wg_{th}_{sec}", [128, 8, 1024], BF16) as wg_sb, \
                            h.sbt(f"wu_{th}_{sec}", [128, 8, 1024], BF16) as wu_sb, \
                            h.pst(f"pg_{th}_{sec}0", [128, 512], F32) as pg0, \
                            h.pst(f"pg_{th}_{sec}1", [128, 512], F32) as pg1, \
                            h.pst(f"pg_{th}_{sec}2", [128, 512], F32) as pg2, \
                            h.pst(f"pu_{th}_{sec}0", [128, 512], F32) as pu0, \
                            h.pst(f"pu_{th}_{sec}1", [128, 512], F32) as pu1, \
                            h.pst(f"pu_{th}_{sec}2", [128, 512], F32) as pu2:
                        h.gdma(wg_sb.ap(), wgT.rearrange("(dc p) e -> p dc e", p=128)
                               [:, :, sec * 1024:(sec + 1) * 1024])
                        h.gdma(wu_sb.ap(), wuT.rearrange("(dc p) e -> p dc e", p=128)
                               [:, :, sec * 1024:(sec + 1) * 1024])
                        h.flush()
                        pgs, pus = [pg0, pg1, pg2], [pu0, pu1, pu2]
                        units = [(hc, ts) for hc in range(8) for ts in range(2)]

                        def mms(i, sec=sec, wg_sb=wg_sb, wu_sb=wu_sb):
                            hc, ts = units[i]
                            tsl = slice(ts * 512, (ts + 1) * 512)
                            for dc in range(8):
                                nc.tensor.matmul(pgs[i % 3].ap(),
                                                 wg_sb.ap()[:, dc, hc * 128:(hc + 1) * 128],
                                                 x1f.ap()[:, dc, tsl],
                                                 start=(dc == 0), stop=(dc == 7))
                            for dc in range(8):
                                ins = nc.tensor.matmul(
                                    pus[i % 3].ap(),
                                    wu_sb.ap()[:, dc, hc * 128:(hc + 1) * 128],
                                    x1f.ap()[:, dc, tsl],
                                    start=(dc == 0), stop=(dc == 7))
                            return ins

                        def st1(u):
                            return nc.scalar.activation(sg[u % 3].ap(), pgs[u % 3].ap(),
                                                        AF.Silu, scale=h.cc("qs"))

                        def st2(u, sec=sec):
                            hc, ts = units[u]
                            tsl = slice(ts * 512, (ts + 1) * 512)
                            return nc.vector.scalar_tensor_tensor(
                                hsb.ap()[:, sec * 8 + hc, tsl], pus[u % 3].ap(),
                                QS * QS, sg[u % 3].ap(), ALU.mult, ALU.mult)

                        h.pipe(len(units), mms, [("scalar", st1), ("vector", st2)], 3, name="qkv")
                with h.pst(f"pd_{th}0", [128, C], F32) as pd0, \
                        h.pst(f"pd_{th}1", [128, C], F32) as pd1:
                    pds = [pd0, pd1]

                    def mms3(i, ccs=ccs):
                        cc = ccs[i]
                        pd = pds[i % 2]
                        for half in range(2):
                            sl = slice(half * 512, (half + 1) * 512)
                            for hc in range(24):
                                nc.tensor.matmul(pd.ap()[:, sl],
                                                 hsb.ap()[:, hc, i * 128:(i + 1) * 128],
                                                 wd.ap()[:, hc, sl],
                                                 start=(hc == 0), stop=False)
                            ins = nc.tensor.matmul(pd.ap()[:, sl], h.ident.ap(),
                                                   x1_tm.ap()[:, cc, sl],
                                                   start=False, stop=True)
                        return ins

                    def epi3(u, ccs=ccs):
                        cc = ccs[u]
                        pd = pds[u % 2]
                        ins = nc.vector.tensor_scalar(
                            zsb.ap()[:, u, :], pd.ap(), 1.0, 0.0,
                            ALU.mult, ALU.add, accum_out=s1t.ap()[:, cc:cc + 1])
                        h.vedge(ins)
                        return nc.vector.scalar_tensor_tensor(
                            junk_a.ap(), zsb.ap()[:, u, :], 1.0,
                            zsb.ap()[:, u, :], ALU.mult, ALU.mult,
                            accum_out=s2t.ap()[:, cc:cc + 1])

                    h.pipe(8, mms3, [("vector", epi3)], 2, name="dn")
                h.ln_batch_stats(s1t.ap()[:, th * 8:(th + 1) * 8],
                                 s2t.ap()[:, th * 8:(th + 1) * 8],
                                 sc1t.ap()[:, th * 8:(th + 1) * 8],
                                 sc2t.ap()[:, th * 8:(th + 1) * 8], 8, C,
                                 h.cc("ln_eps"))
                for i, cc in enumerate(ccs):
                    nc.vector.tensor_scalar(x1_tm.ap()[:, cc, :], zsb.ap()[:, i, :],
                                            sc1t.ap()[:, cc:cc + 1],
                                            sc2t.ap()[:, cc:cc + 1],
                                            ALU.subtract, ALU.mult)
                h.bar()
        h.dma(o_x2.rearrange("(cc p) f -> p cc f", p=128), x1_tm.ap())
        if halo_in is not None:
            h.dma(halo_in, x1_tm.ap()[:, NCH - 1, :])
        h.flush()


NCH_IN = 17
DBG_D = False


def build_D():
    nc = _nc()
    x2in = nc.dram_tensor("x2in", [NCH_IN * 128, C], BF16, kind="ExternalInput").ap()
    wqkvT = nc.dram_tensor("wqkvT", [C, 3 * C], F32, kind="ExternalInput").ap()
    lnw = nc.dram_tensor("lnw", [C], F32, kind="ExternalInput").ap()
    lnb = nc.dram_tensor("lnb", [C], F32, kind="ExternalInput").ap()
    woutT = nc.dram_tensor("woutT", [C, C], F32, kind="ExternalInput").ap()
    cost = nc.dram_tensor("cost", [NCH_IN * 128, 32], F32, kind="ExternalInput").ap()
    ssint = nc.dram_tensor("ssint", [NCH_IN * 128, 32], F32, kind="ExternalInput").ap()
    mown = nc.dram_tensor("mown", [128, 160], F32, kind="ExternalInput").ap()
    mhalo = nc.dram_tensor("mhalo", [128, 160], F32, kind="ExternalInput").ap()
    o_x3 = nc.dram_tensor("x3", [TOK, C], BF16, kind="ExternalOutput").ap()
    with ExitStack() as gst:
        h = H(nc, gst)
        h.mk_ident()
        h.mk_consts(ALL_CONSTS)
        emit_D(nc, h, x2in, None, None, wqkvT, lnw, lnb, woutT, cost, ssint,
               mown, mhalo, o_x3)
    return nc


def emit_D(nc, h, x2in, x2_own, halo_out, wqkvT, lnw, lnb, woutT, cost, ssint,
           mown, mhalo, o_x3, await_cb=None):
    with ExitStack() as st:
        x2_tm = st.enter_context(h.sbt("x2_tm", [128, NCH_IN, C], BF16))
        ct = st.enter_context(h.sbt("ct", [128, NCH_IN, 32], F32))
        sst = st.enter_context(h.sbt("sst", [128, NCH_IN, 32], F32))
        gam = st.enter_context(h.sbt("gam", [128, 8], F32))
        s1t = st.enter_context(h.sbt("s1t", [128, NCH_IN], F32))
        s2t = st.enter_context(h.sbt("s2t", [128, NCH_IN], F32))
        sc1t = st.enter_context(h.sbt("sc1t", [128, NCH_IN], F32))
        sc2t = st.enter_context(h.sbt("sc2t", [128, NCH_IN], F32))
        rec = st.enter_context(h.sbt("rec", [128, HEADS], F32))
        radd = st.enter_context(h.sbt("radd", [128, HEADS], F32))
        atrow = [st.enter_context(h.sbt(f"atrow{j}", [128, C], BF16)) for j in range(2)]
        junk_a = atrow[0]
        st2 = ExitStack()
        qfm = st2.enter_context(h.sbt("qfm", [128, 8, NCH_IN * 128], BF16))
        kfm = st2.enter_context(h.sbt("kfm", [128, 8, NCH_IN * 128], BF16))
        vex = st2.enter_context(h.sbt("vex", [128, NCH_IN, HEADS * 65], BF16))

        if x2in is not None:
            h.gdma(x2_tm.ap(), x2in.rearrange("(cc p) f -> p cc f", p=128))
        else:
            h.gdma(x2_tm.ap()[:, 1:NCH_IN, :],
                   x2_own.rearrange("(cc p) f -> p cc f", p=128))
        h.gdma(ct.ap(), cost.rearrange("(cc p) f -> p cc f", p=128))
        h.gdma(sst.ap(), ssint.rearrange("(cc p) f -> p cc f", p=128))
        if x2in is None:
            if await_cb is not None:
                await_cb()
            h.gdma(x2_tm.ap()[:, 0, :],
                   halo_out.rearrange("(cc p) f -> p cc f", p=128)[:, 0, :])
        h.flush()
        nc.vector.memset(vex.ap().rearrange("p cc (hd f) -> p cc hd f", f=65)[:, :, :, 64:65], 1.0)
        for cc in range(NCH_IN):
            ins = nc.vector.tensor_scalar(junk_a.ap(), x2_tm.ap()[:, cc, :], 1.0, 0.0,
                                          ALU.mult, ALU.add,
                                          accum_out=s1t.ap()[:, cc:cc + 1])
            h.vedge(ins)
            nc.vector.scalar_tensor_tensor(junk_a.ap(), x2_tm.ap()[:, cc, :], 1.0,
                                           x2_tm.ap()[:, cc, :], ALU.mult, ALU.mult,
                                           accum_out=s2t.ap()[:, cc:cc + 1])
            h.vedge(nc.vector.engine_nop())
        h.bar()
        h.ln_batch_stats(s1t.ap(), s2t.ap(), sc1t.ap(), sc2t.ap(), NCH_IN, C,
                         h.cc("ln_eps"))

        with h.sbt("wq", [128, 8, 3 * C], BF16) as wq, \
                h.sbt("bq", [1, 3 * C], BF16) as bq, \
                h.sbt("bet", [128, 8], BF16) as bet:
            h.gdma(wq.ap(), wqkvT.rearrange("(dc p) e -> p dc e", p=128))
            with nc.allow_non_contiguous_dma(reason="tiny 1k-element vector loads"):
                h.gdma(gam.ap(), lnw.rearrange("(dc p) -> p dc", p=128))
                h.gdma(bet.ap(), lnb.rearrange("(dc p) -> p dc", p=128))
            h.flush()
            with h.pst("pb", [1, 3 * C], F32) as pb:
                for dc in range(8):
                    for ns in range(6):
                        nc.tensor.matmul(pb.ap()[:, ns * 512:(ns + 1) * 512],
                                         bet.ap()[:, dc:dc + 1],
                                         wq.ap()[:, dc, ns * 512:(ns + 1) * 512],
                                         start=(dc == 0), stop=(dc == 7))
                h.bar()
                nc.vector.tensor_copy(bq.ap(), pb.ap())
                h.bar()
            for dc in range(8):
                nc.vector.tensor_scalar_mul(wq.ap()[:, dc, :], wq.ap()[:, dc, :],
                                            gam.ap()[:, dc:dc + 1])
            h.bar()

            with h.sbt("xn_c", [128, C], BF16) as xn_c, \
                    h.sbt("xn_f", [128, 8, 128], BF16) as xn_f, \
                    h.sbt("rr", [128, 512], BF16) as rr, \
                    h.sbt("rt", [128, 512], BF16) as rt, \
                    h.pst("ps_ta", [128, 8, 128], BF16) as ps_ta, \
                    h.pst("ps_tb", [128, 8, 128], BF16) as ps_tb, \
                    h.pst("ps_q0", [128, 512], F32) as ps_q0, \
                    h.pst("ps_q1", [128, 512], F32) as ps_q1:
                class _V:
                    def __init__(self, a):
                        self._a = a

                    def ap(self):
                        return self._a
                rds = [_V(atrow[1].ap()[:, 0:512]), _V(atrow[1].ap()[:, 512:1024])]
                pqs = [ps_q0, ps_q1]
                pts = [ps_ta, ps_tb]
                for cc in range(NCH_IN):
                    nc.vector.tensor_scalar(xn_c.ap(), x2_tm.ap()[:, cc, :],
                                            sc1t.ap()[:, cc:cc + 1],
                                            sc2t.ap()[:, cc:cc + 1],
                                            ALU.subtract, ALU.mult)
                    h.bar()
                    h.pipe_transpose(
                        [([xn_c.ap()[:, dc * 128:(dc + 1) * 128] for dc in range(4)],
                          xn_f.ap()[:, 0:4, :]),
                         ([xn_c.ap()[:, dc * 128:(dc + 1) * 128] for dc in range(4, 8)],
                          xn_f.ap()[:, 4:8, :])],
                        ps_ta, ps_tb)
                    for i in range(6 + 3):
                        if i < 6:
                            es = i
                            esl = slice(es * 512, (es + 1) * 512)
                            for dc in range(8):
                                nc.tensor.matmul(pqs[i % 2].ap(), xn_f.ap()[:, dc, :],
                                                 wq.ap()[:, dc, esl],
                                                 start=(dc == 0), stop=False)
                            nc.tensor.matmul(pqs[i % 2].ap(), h.ones_row.ap(),
                                             bq.ap()[:, esl], start=False, stop=True)
                        if 1 <= i <= 6:
                            es = i - 1
                            pq = pqs[es % 2]
                            if es < 4:
                                ca = ct.ap()[:, cc, :]
                                sa = sst.ap()[:, cc, :]
                                cb = bass.AP(ca.tensor, ca.offset,
                                             [list(ca.ap[0])] + [[0, 8], [0, 2], [1, 32]])
                                sb = bass.AP(sa.tensor, sa.offset,
                                             [list(sa.ap[0])] + [[0, 8], [1, 32]])
                                qv = pq.ap().rearrange("p (hd two f) -> p hd two f",
                                                       two=2, f=32)
                                rr2 = rr.ap().rearrange("p (hd two f) -> p hd two f",
                                                        two=2, f=32)
                                rt2 = rt.ap().rearrange("p (hd two f) -> p hd two f",
                                                        two=2, f=32)
                                nc.vector.tensor_tensor(rr2, qv, cb, ALU.mult)
                                nc.vector.tensor_tensor(rt2[:, :, 0, :], qv[:, :, 1, :],
                                                        sb, ALU.mult)
                                ins = nc.vector.tensor_tensor(rt2[:, :, 1, :],
                                                              qv[:, :, 0, :], sb,
                                                              ALU.mult)
                                h.vedge(ins)
                                d2 = rds[es % 2].ap().rearrange(
                                    "p (hd two f) -> p hd two f", two=2, f=32)
                                nc.vector.tensor_tensor(d2[:, :, 0, :], rr2[:, :, 0, :],
                                                        rt2[:, :, 0, :], ALU.subtract)
                                nc.vector.tensor_tensor(d2[:, :, 1, :], rr2[:, :, 1, :],
                                                        rt2[:, :, 1, :], ALU.add)
                            else:
                                nc.scalar.copy(
                                    vex.ap().rearrange("p cc2 (hd f) -> p cc2 hd f", f=65)
                                    [:, cc, (es - 4) * 8:(es - 4) * 8 + 8, 0:64],
                                    pq.ap().rearrange("p (hd f) -> p hd f", f=64))
                        if 2 <= i <= 7:
                            es = i - 2
                            if es < 4:
                                for dc4 in range(4):
                                    nc.tensor.transpose(
                                        pts[es % 2].ap()[:, dc4, :],
                                        rds[es % 2].ap()[:, dc4 * 128:(dc4 + 1) * 128],
                                        h.ident.ap())
                        if 3 <= i <= 8:
                            es = i - 3
                            if es < 4:
                                tgt = qfm if es < 2 else kfm
                                base = (es % 2) * 4
                                nc.scalar.copy(
                                    tgt.ap()[:, base:base + 4, cc * 128:(cc + 1) * 128],
                                    pts[es % 2].ap()[:, 0:4, :])
                        h.bar()

        attn_d = nc.dram_tensor(h.prefix + "attn_d", [TOK, C], BF16, kind="Internal").ap()
        with h.sbt("m_own", [128, 160], F32) as m_own, \
                h.sbt("m_halo", [128, 160], F32) as m_halo, \
                h.pst("ps_sA", [128, 4, 256], F32) as ps_sA, \
                h.pst("ps_sB", [128, 4, 256], F32) as ps_sB, \
                h.pst("po", [128, HEADS, 128], F32) as po, \
                h.sbt("p_a", [128, HEADS, 160], BF16) as p_a, \
                h.sbt("p_b", [128, HEADS, 160], BF16) as p_b:
            h.dma(m_own.ap(), mown)
            h.dma(m_halo.ap(), mhalo)
            h.flush()
            pbufs = [p_a, p_b]
            for kc in range(NCH_IN):
                p_cur = pbufs[kc % 2]
                p_prev = pbufs[1 - kc % 2]
                qlo = kc * 128
                qn = min(160, NCH_IN * 128 - qlo)
                mask = m_halo if kc == 0 else m_own
                for hh in range(2):
                    for i4 in range(4):
                        for par, pss in ((0, ps_sA), (1, ps_sB)):
                            hd = hh * 8 + 2 * i4 + par
                            hp = par * 64
                            nc.tensor.matmul(pss.ap()[:, i4, 0:qn],
                                             kfm.ap()[hp:hp + 64, hd // 2, kc * 128:(kc + 1) * 128],
                                             qfm.ap()[hp:hp + 64, hd // 2, qlo:qlo + qn],
                                             start=True, stop=True,
                                             tile_position=(hp, 0))
                    h.bar()
                    for pss in (ps_sA, ps_sB):
                        nc.scalar.activation(pss.ap()[:, :, 0:qn], pss.ap()[:, :, 0:qn],
                                             AF.Exp, bias=h.cc("negC"), scale=h.cc("s8"))
                    h.bar()
                    for par, pss in ((0, ps_sA), (1, ps_sB)):
                        pv = p_cur.ap()[:, hh * 8 + par:hh * 8 + 8:2, 0:qn]
                        nc.vector.tensor_tensor(pv, pss.ap()[:, :, 0:qn],
                                                _ap_bcast(mask.ap()[:, 0:qn], 4),
                                                ALU.mult)
                    h.bar()
                if kc == 0:
                    continue
                oc = kc
                for hd in range(HEADS):
                    nc.tensor.matmul(po.ap()[:, hd, 0:65],
                                     p_cur.ap()[:, hd, 0:128],
                                     vex.ap().rearrange("p cc2 (h2 f) -> p cc2 h2 f", f=65)
                                     [:, kc, hd, :],
                                     start=True, stop=False)
                    nc.tensor.matmul(po.ap()[0:32, hd, 0:65],
                                     p_prev.ap()[:, hd, 128:160],
                                     vex.ap().rearrange("p cc2 (h2 f) -> p cc2 h2 f", f=65)
                                     [:, kc - 1, hd, :],
                                     start=False, stop=True)
                h.bar()
                ins = nc.vector.tensor_scalar(radd.ap(), po.ap()[:, :, 64:65].squeeze(2),
                                              1e-30, 0.0, ALU.add, ALU.add)
                h.vedge(ins)
                ins = nc.vector.reciprocal(rec.ap(), radd.ap())
                h.vedge(ins)
                nc.vector.tensor_tensor(
                    atrow[oc % 2].ap().rearrange("p (hd f) -> p hd f", f=64),
                    po.ap()[:, :, 0:64], _ap_bcast_last(rec.ap(), 64), ALU.mult)
                h.bar()
                h.dma(attn_d.rearrange("(c2 p) f -> p c2 f", p=128)[:, oc - 1, :],
                      atrow[oc % 2].ap())
                if oc >= 2:
                    h.dwait()

        st2.close()
        with h.sbt("attn_fm", [128, 8, TOK], BF16) as attn_fm, \
                h.sbt("wo", [128, 8, C], BF16) as wo, \
                h.sbt("z_sb", [128, NCH, C], BF16) as z_sb, \
                h.pst("ps_ta3", [128, 8, 128], BF16) as ps_ta3, \
                h.pst("ps_tb3", [128, 8, 128], BF16) as ps_tb3, \
                h.pst("pz0", [128, C], F32) as pz0, \
                h.pst("pz1", [128, C], F32) as pz1:
            h.gdma(wo.ap(), woutT.rearrange("(dc p) o -> p dc o", p=128))
            for cc in range(NCH):
                h.dma(atrow[cc % 2].ap(),
                      attn_d.rearrange("(c2 p) f -> p c2 f", p=128)[:, cc, :])
                h.flush()
                h.pipe_transpose(
                    [([atrow[cc % 2].ap()[:, dc * 128:(dc + 1) * 128] for dc in range(4)],
                      attn_fm.ap()[:, 0:4, cc * 128:(cc + 1) * 128]),
                     ([atrow[cc % 2].ap()[:, dc * 128:(dc + 1) * 128] for dc in range(4, 8)],
                      attn_fm.ap()[:, 4:8, cc * 128:(cc + 1) * 128])],
                    ps_ta3, ps_tb3)
            pzs = [pz0, pz1]
            for i in range(NCH + 1):
                if i < NCH:
                    pz = pzs[i % 2]
                    for half in range(2):
                        sl = slice(half * 512, (half + 1) * 512)
                        for dc in range(8):
                            nc.tensor.matmul(pz.ap()[:, sl],
                                             attn_fm.ap()[:, dc, i * 128:(i + 1) * 128],
                                             wo.ap()[:, dc, sl],
                                             start=(dc == 0), stop=False)
                        nc.tensor.matmul(pz.ap()[:, sl], h.ident2.ap(),
                                         x2_tm.ap()[:, i + 1, sl], start=False, stop=True)
                if i > 0:
                    cc = i - 1
                    pz = pzs[cc % 2]
                    ins = nc.vector.tensor_scalar(z_sb.ap()[:, cc, :], pz.ap(),
                                                  1.0, 0.0, ALU.mult, ALU.add,
                                                  accum_out=s1t.ap()[:, cc:cc + 1])
                    h.vedge(ins)
                    nc.vector.scalar_tensor_tensor(junk_a.ap(), z_sb.ap()[:, cc, :],
                                                   1.0, z_sb.ap()[:, cc, :],
                                                   ALU.mult, ALU.mult,
                                                   accum_out=s2t.ap()[:, cc:cc + 1])
                h.bar()
            h.ln_batch_stats(s1t.ap(), s2t.ap(), sc1t.ap(), sc2t.ap(), NCH, C,
                             h.cc("ln_eps"))
            for cc in range(NCH):
                nc.vector.tensor_scalar(z_sb.ap()[:, cc, :].bitcast(BF16),
                                        z_sb.ap()[:, cc, :],
                                        sc1t.ap()[:, cc:cc + 1], sc2t.ap()[:, cc:cc + 1],
                                        ALU.subtract, ALU.mult)
            h.bar()
            h.dma(o_x3.rearrange("(cc p) f -> p cc f", p=128), z_sb.ap())
            h.flush()


GRP = 2


def build_E():
    nc = _nc()
    x3f = nc.dram_tensor("x3f", [T, C], BF16, kind="ExternalInput").ap()
    wq_d = nc.dram_tensor("wq", [C, 512], F32, kind="ExternalInput").ap()
    wk_d = nc.dram_tensor("wk", [C, 512], F32, kind="ExternalInput").ap()
    wv_d = nc.dram_tensor("wv", [C, 512], F32, kind="ExternalInput").ap()
    o_at = nc.dram_tensor("attn", [512, T], BF16, kind="ExternalOutput").ap()
    with ExitStack() as gst:
        h = H(nc, gst)
        h.mk_ident()
        h.mk_consts(ALL_CONSTS)
        emit_E(nc, h, x3f, wq_d, wk_d, wv_d, o_at, a2a_mode=False)
    return nc


def emit_E(nc, h, x3f, wq_d, wk_d, wv_d, o_at, a2a_mode, await_cb=None):
    with ExitStack() as st:
        qf = st.enter_context(h.sbt("qf", [128, 4, T], BF16))
        kf = st.enter_context(h.sbt("kf", [128, 4, T], BF16))
        vt = st.enter_context(h.sbt("vt", [128, 32, 512], BF16))

        with h.sbt("wqs", [128, 8, 512], BF16) as wqs, \
                h.sbt("wks", [128, 8, 512], BF16) as wks, \
                h.sbt("wvs", [128, 8, 512], BF16) as wvs:
            h.gdma(wqs.ap(), wq_d.rearrange("(dc p) e -> p dc e", p=128))
            h.gdma(wks.ap(), wk_d.rearrange("(dc p) e -> p dc e", p=128))
            h.gdma(wvs.ap(), wv_d.rearrange("(dc p) e -> p dc e", p=128))
            if await_cb is not None:
                await_cb()
            h.flush()
            for th in range(4):
                with h.sbt(f"x_tm_{th}", [128, 8, C], BF16) as x_tm, \
                        h.sbt(f"x_fm_{th}", [128, 8, 1024], BF16) as x_fm, \
                        h.pst(f"ps_t_{th}", [128, 8, 128], BF16) as ps_t, \
                        h.pst(f"ps_m_{th}", [128, 512], F32) as ps_m:
                    h.gdma(x_tm.ap(), x3f.rearrange("(cc p) f -> p cc f", p=128)
                           [:, th * 8:(th + 1) * 8, :])
                    h.flush()
                    for cc in range(8):
                        h.transpose_group(
                            ps_t,
                            [x_tm.ap()[:, cc, dc * 128:(dc + 1) * 128] for dc in range(8)],
                            x_fm.ap()[:, :, cc * 128:(cc + 1) * 128])
                    units = []
                    for lc in range(4):
                        for ts in range(2):
                            for w_sb, tgt in ((wqs, qf), (wks, kf)):
                                units.append((
                                    [w_sb.ap()[:, dc, lc * 128:(lc + 1) * 128]
                                     for dc in range(8)],
                                    [x_fm.ap()[:, dc, ts * 512:(ts + 1) * 512]
                                     for dc in range(8)],
                                    tgt.ap()[:, lc, th * 1024 + ts * 512:
                                             th * 1024 + (ts + 1) * 512]))
                    for cc in range(8):
                        units.append((
                            [x_fm.ap()[:, dc, cc * 128:(cc + 1) * 128] for dc in range(8)],
                            [wvs.ap()[:, dc, :] for dc in range(8)],
                            vt.ap()[:, th * 8 + cc, :]))
                    with h.pst(f"ps_m2_{th}", [128, 512], F32) as ps_m2:
                        pms = [ps_m, ps_m2]
                        n = len(units)
                        for i in range(n + 1):
                            if i < n:
                                ls, rs, _ = units[i]
                                for dc in range(8):
                                    nc.tensor.matmul(pms[i % 2].ap(), ls[dc], rs[dc],
                                                     start=(dc == 0), stop=(dc == 7))
                            if i > 0:
                                _, _, dst = units[i - 1]
                                nc.scalar.copy(dst, pms[(i - 1) % 2].ap())
                            h.bar()

        acc_a = st.enter_context(h.sbt("acc_a", [128, T], F32))
        acc_b = st.enter_context(h.sbt("acc_b", [128, T], F32))
        at_sb = st.enter_context(h.sbt("at_sb", [128, T], BF16))
        zp = st.enter_context(h.sbt("zp", [128, 2, GRP, 8], F32))
        zs = st.enter_context(h.sbt("zs", [128, 2, GRP, 2], F32))
        rz = st.enter_context(h.sbt("rz", [128, 2, GRP, 2], F32))
        rzb = st.enter_context(h.sbt("rzb", [128, 2, GRP, 2], BF16))
        with h.sbt("p_sb", [128, 2, GRP, 2, T], BF16) as p_sb, \
                h.sbt("w_sb", [128, 2, GRP, 128], BF16) as w_sb, \
                h.pst("ps_s0", [128, 1024], F32) as ps_s0, \
                h.pst("ps_s1", [128, 1024], F32) as ps_s1, \
                h.pst("ps_o0", [128, 512], F32) as ps_o0, \
                h.pst("ps_o1", [128, 512], F32) as ps_o1:
            smm, sexp = h.sem("smm"), h.sem("sexp")
            somm, soac = h.sem("somm"), h.sem("soac")
            swp = h.sem("swp")
            sbufs = [ps_s0, ps_s1]
            obufs = [ps_o0, ps_o1]
            n_mm = n_exp = n_omm = n_oac = n_wp = 0
            n_g = 32 // GRP
            exp_done = {}

            for pair in range(4):
                for g in range(n_g + 1):
                    if g > 0:
                        gp = g - 1
                        gg = gp % 2
                        nc.vector.wait_ge(sexp, exp_done[(pair, gp)])
                        for si in range(GRP):
                            ins = nc.vector.reduce_sum(
                                zs.ap()[:, gg, si, :],
                                zp.ap()[:, gg, si, :].rearrange(
                                    "p (h2 f) -> p h2 f", f=4), axis=AX.X)
                        h.vedge(ins)
                        ins = nc.vector.reciprocal(
                            rz.ap()[:, gg].rearrange("p a b -> p (a b)"),
                            zs.ap()[:, gg].rearrange("p a b -> p (a b)"))
                        h.vedge(ins)
                        ins = nc.vector.tensor_scalar(
                            rzb.ap()[:, gg].rearrange("p a b -> p (a b)"),
                            rz.ap()[:, gg].rearrange("p a b -> p (a b)"),
                            1.0, 0.0, ALU.mult, ALU.add)
                        h.vedge(ins)
                        for si in range(GRP):
                            s = gp * GRP + si
                            ins = nc.vector.tensor_tensor(
                                w_sb.ap()[:, gg, si, :].rearrange(
                                    "p (h2 f) -> p h2 f", f=64),
                                vt.ap()[:, s, pair * 128:(pair + 1) * 128]
                                .rearrange("p (h2 f) -> p h2 f", f=64),
                                _ap_bcast_last(rzb.ap()[:, gg, si, :], 64), ALU.mult)
                        ins.then_inc(swp, 1)
                        n_wp += 1
                        nc.tensor.wait_ge(swp, n_wp)

                    def emit_out_qs(gp, qs):
                        nonlocal n_omm, n_oac
                        gg = gp % 2
                        obuf = obufs[n_omm % 2]
                        if n_omm >= 2:
                            nc.tensor.wait_ge(soac, n_omm - 1)
                        for si in range(GRP):
                            nc.tensor.matmul(obuf.ap()[0:64, :],
                                             w_sb.ap()[:, gg, si, 0:64],
                                             p_sb.ap()[:, gg, si, 0,
                                                       qs * 512:(qs + 1) * 512],
                                             start=(si == 0), stop=False,
                                             tile_position=(0, 0))
                            ins = nc.tensor.matmul(
                                obuf.ap()[64:128, :],
                                w_sb.ap()[:, gg, si, 64:128],
                                p_sb.ap()[:, gg, si, 1, qs * 512:(qs + 1) * 512],
                                start=(si == 0), stop=(si == GRP - 1),
                                tile_position=(0, 64))
                        ins.then_inc(somm, 1)
                        n_omm += 1
                        nc.vector.wait_ge(somm, n_omm)
                        qsl = slice(qs * 512, (qs + 1) * 512)
                        if gp == 0:
                            ins = nc.vector.tensor_scalar(
                                acc_a.ap()[:, qsl], obuf.ap(),
                                1.0, 0.0, ALU.mult, ALU.add)
                        elif gp % 2 == 1:
                            ins = nc.vector.tensor_add(acc_b.ap()[:, qsl],
                                                       obuf.ap(),
                                                       acc_a.ap()[:, qsl])
                        else:
                            ins = nc.vector.tensor_add(acc_a.ap()[:, qsl],
                                                       obuf.ap(),
                                                       acc_b.ap()[:, qsl])
                        ins.then_inc(soac, 1)
                        n_oac += 1

                    if g < n_g:
                        gg = g % 2
                        rnd = 0
                        for si in range(GRP):
                            s = g * GRP + si
                            for hi in range(2):
                                hp = hi * 64
                                for qp in range(4):
                                    buf = sbufs[n_mm % 2]
                                    if n_mm >= 2:
                                        nc.tensor.wait_ge(sexp, n_mm - 1)
                                    for half in range(2):
                                        q0 = (qp * 2 + half) * 512
                                        ins = nc.tensor.matmul(
                                            buf.ap()[:, half * 512:(half + 1) * 512],
                                            kf.ap()[hp:hp + 64, pair, s * 128:(s + 1) * 128],
                                            qf.ap()[hp:hp + 64, pair, q0:q0 + 512],
                                            start=True, stop=True,
                                            tile_position=(hp, 0))
                                    ins.then_inc(smm, 1)
                                    n_mm += 1
                                    nc.scalar.wait_ge(smm, n_mm)
                                    nc.scalar.activation(
                                        p_sb.ap()[:, gg, si, hi, qp * 1024:(qp + 1) * 1024],
                                        buf.ap(), AF.Exp, scale=h.cc("s32"),
                                        accum_out=zp.ap()[:, gg, si, hi * 4 + qp]
                                        .unsqueeze(1)).then_inc(sexp, 1)
                                    n_exp += 1
                                    if g > 0 and rnd % 2 == 0:
                                        emit_out_qs(g - 1, rnd // 2)
                                    rnd += 1
                        exp_done[(pair, g)] = n_exp
                    else:
                        for qs in range(8):
                            emit_out_qs(g - 1, qs)
                fin = acc_a if (n_g - 1) % 2 == 0 else acc_b
                h.bar()
                nc.vector.tensor_scalar(at_sb.ap(), fin.ap(), 1.0, 0.0, ALU.mult, ALU.add)
                h.bar()
                if a2a_mode:
                    av = o_at.rearrange("(s2 pr p) t -> p s2 pr t", s2=2, p=128)
                    for s2 in range(2):
                        h.dma(av[:, s2, pair, :], at_sb.ap()[:, s2 * TOK:(s2 + 1) * TOK])
                else:
                    h.dma(o_at.rearrange("(pr p) t -> p pr t", p=128)[:, pair, :],
                          at_sb.ap())
                h.flush()
        h.flush()


def build_F():
    nc = _nc()
    at_d = nc.dram_tensor("attn_own", [C, TOK], BF16, kind="ExternalInput").ap()
    x3_d = nc.dram_tensor("x3own", [TOK, C], BF16, kind="ExternalInput").ap()
    wo_d = nc.dram_tensor("wo", [C, C], F32, kind="ExternalInput").ap()
    wff_d = nc.dram_tensor("wff", [C, C], F32, kind="ExternalInput").ap()
    o_x4 = nc.dram_tensor("x4", [TOK, C], F32, kind="ExternalOutput").ap()
    with ExitStack() as gst:
        h = H(nc, gst)
        h.mk_ident()
        h.mk_consts(ALL_CONSTS)
        emit_F(nc, h, at_d, x3_d, wo_d, wff_d, o_x4)
    return nc


def emit_F(nc, h, at_d, x3_d, wo_d, wff_d, o_x4, sel=None, await_cb=None):
    with ExitStack() as st:
        x3_tm = st.enter_context(h.sbt("x3_tm", [128, NCH, C], BF16))
        wo = st.enter_context(h.sbt("wo_sb", [128, 8, C], BF16))
        wff = st.enter_context(h.sbt("wff_sb", [128, 8, C], BF16))
        zA = st.enter_context(h.sbt("zA", [128, NCH, C], BF16))
        zB = st.enter_context(h.sbt("zB", [128, NCH, C], BF16))
        s1t = st.enter_context(h.sbt("s1t", [128, NCH], F32))
        s2t = st.enter_context(h.sbt("s2t", [128, NCH], F32))
        sc1t = st.enter_context(h.sbt("sc1t", [128, NCH], F32))
        sc2t = st.enter_context(h.sbt("sc2t", [128, NCH], F32))
        r1t = st.enter_context(h.sbt("r1t", [128, NCH], F32))
        r2t = st.enter_context(h.sbt("r2t", [128, NCH], F32))
        junk_a = st.enter_context(h.sbt("junk_a", [128, C], BF16))

        h.gdma(x3_tm.ap(), x3_d.rearrange("(cc p) f -> p cc f", p=128))
        h.gdma(wo.ap(), wo_d.rearrange("(lc p) d -> p lc d", p=128))
        h.gdma(wff.ap(), wff_d.rearrange("(dc p) d -> p dc d", p=128))
        if await_cb is not None:
            await_cb()
        h.flush()

        def l2_batch(s2, rt, n):
            nc.scalar.activation(s2[:, 0:n], s2[:, 0:n], AF.Sqrt)
            h.bar()
            nc.vector.tensor_scalar_max(s2[:, 0:n], s2[:, 0:n], L2_EPS)
            h.bar()
            nc.vector.reciprocal(rt[:, 0:n], s2[:, 0:n])
            h.bar()

        def zstats(dst_ap, ps, s1col, s2col, with_s1):
            if with_s1:
                ins = nc.vector.tensor_scalar(dst_ap, ps.ap(), 1.0, 0.0,
                                              ALU.mult, ALU.add, accum_out=s1col)
            else:
                ins = nc.vector.tensor_scalar(dst_ap, ps.ap(), 1.0, 0.0,
                                              ALU.mult, ALU.add)
            h.vedge(ins)
            nc.vector.scalar_tensor_tensor(junk_a.ap(), dst_ap, 1.0, dst_ap,
                                           ALU.mult, ALU.mult, accum_out=s2col)

        with h.sbt("atf", [128, 8, TOK], BF16) as atf, \
                h.sbt("znf", [128, 8, TOK], BF16) as znf:
            if sel is None:
                h.gdma(atf.ap(), at_d.rearrange("(lc p) t -> p lc t", p=128))
                h.flush()
            else:
                with h.sbt("sel_sb", [128, 2], F32) as sel_sb, \
                        h.sbt("atA", [128, TOK], BF16) as atA, \
                        h.sbt("atB", [128, TOK], BF16) as atB, \
                        h.sbt("atT", [128, TOK], BF16) as atT:
                    h.dma(sel_sb.ap(), sel)
                    agv = at_d.rearrange("(lc p) t -> p lc t", p=128)
                    for lc in range(8):
                        h.gdma(atA.ap(), agv[:, lc, 0:TOK])
                        h.gdma(atB.ap(), agv[:, lc, TOK:2 * TOK])
                        h.flush()
                        ins = nc.vector.tensor_scalar(atT.ap(), atA.ap(),
                                                      sel_sb.ap()[:, 0:1], None, ALU.mult)
                        h.vedge(ins)
                        nc.vector.scalar_tensor_tensor(atf.ap()[:, lc, :], atB.ap(),
                                                       sel_sb.ap()[:, 1:2], atT.ap(),
                                                       ALU.mult, ALU.add)
                        h.flush()
            with h.pst("pz0", [128, C], F32) as pz0, h.pst("pz1", [128, C], F32) as pz1:
                pzs = [pz0, pz1]
                for i in range(NCH + 1):
                    if i < NCH:
                        pz = pzs[i % 2]
                        for half in range(2):
                            sl = slice(half * 512, (half + 1) * 512)
                            for lc in range(8):
                                nc.tensor.matmul(pz.ap()[:, sl],
                                                 atf.ap()[:, lc, i * 128:(i + 1) * 128],
                                                 wo.ap()[:, lc, sl],
                                                 start=(lc == 0), stop=(lc == 7))
                    if i > 0:
                        zstats(zA.ap()[:, i - 1, :], pzs[(i - 1) % 2], None,
                               s2t.ap()[:, i - 1:i], with_s1=False)
                    h.bar()
            l2_batch(s2t.ap(), r1t.ap(), NCH)
            zn_tm = atf.ap().rearrange("p qc (a b) -> p (qc a) b", b=C)
            for cc in range(NCH):
                nc.vector.tensor_scalar(zn_tm[:, cc, :], zA.ap()[:, cc, :],
                                        r1t.ap()[:, cc:cc + 1], None, ALU.mult)
            h.bar()
            with h.pst("ps_ta", [128, 8, 128], BF16) as ps_ta, \
                    h.pst("ps_tb", [128, 8, 128], BF16) as ps_tb:
                h.pipe_transpose(
                    [([zn_tm[:, cc, dc * 128:(dc + 1) * 128] for dc in range(8)],
                      znf.ap()[:, :, cc * 128:(cc + 1) * 128]) for cc in range(NCH)],
                    ps_ta, ps_tb)
            with h.pst("pw0", [128, C], F32) as pw0, h.pst("pw1", [128, C], F32) as pw1:
                pws = [pw0, pw1]
                for i in range(NCH + 1):
                    if i < NCH:
                        pw = pws[i % 2]
                        for half in range(2):
                            sl = slice(half * 512, (half + 1) * 512)
                            for dc in range(8):
                                nc.tensor.matmul(pw.ap()[:, sl],
                                                 znf.ap()[:, dc, i * 128:(i + 1) * 128],
                                                 wff.ap()[:, dc, sl],
                                                 start=(dc == 0), stop=(dc == 7))
                    if i > 0:
                        zstats(zB.ap()[:, i - 1, :], pws[(i - 1) % 2], None,
                               s2t.ap()[:, i - 1:i], with_s1=False)
                    h.bar()
            l2_batch(s2t.ap(), r2t.ap(), NCH)

        gb = [st.enter_context(h.sbt(f"gb{j}", [128, C], BF16)) for j in range(2)]
        with h.sbt("z3", [128, 8, C], BF16) as z3, \
                h.sbt("x4o", [128, 8, C], F32) as x4o, \
                h.pst("pr0", [128, C], F32) as pr0, h.pst("pr1", [128, C], F32) as pr1:
            prs = [pr0, pr1]
            for th in range(2):
                ccs = list(range(th * 8, (th + 1) * 8))
                for i in range(8 + 2):
                    if i < 8:
                        cc = ccs[i]
                        nc.scalar.activation(gb[i % 2].ap(), zB.ap()[:, cc, :],
                                             AF.Gelu, scale=r2t.ap()[:, cc:cc + 1])
                    if 1 <= i <= 8:
                        cc = ccs[i - 1]
                        pr = prs[(i - 1) % 2]
                        for half in range(2):
                            sl = slice(half * 512, (half + 1) * 512)
                            nc.tensor.matmul(pr.ap()[:, sl], h.ident.ap(),
                                             x3_tm.ap()[:, cc, sl],
                                             start=True, stop=False)
                            nc.tensor.matmul(pr.ap()[:, sl], h.ident.ap(),
                                             gb[(i - 1) % 2].ap()[:, sl],
                                             start=False, stop=True)
                    if 2 <= i <= 9:
                        cc = ccs[i - 2]
                        zstats(z3.ap()[:, i - 2, :], prs[(i - 2) % 2],
                               s1t.ap()[:, cc:cc + 1], s2t.ap()[:, cc:cc + 1],
                               with_s1=True)
                    h.bar()
                h.ln_batch_stats(s1t.ap()[:, th * 8:(th + 1) * 8],
                                 s2t.ap()[:, th * 8:(th + 1) * 8],
                                 sc1t.ap()[:, th * 8:(th + 1) * 8],
                                 sc2t.ap()[:, th * 8:(th + 1) * 8], 8, C,
                                 h.cc("ln_eps"))
                for i, cc in enumerate(ccs):
                    nc.vector.tensor_scalar(x4o.ap()[:, i, :], z3.ap()[:, i, :],
                                            sc1t.ap()[:, cc:cc + 1],
                                            sc2t.ap()[:, cc:cc + 1],
                                            ALU.subtract, ALU.mult)
                h.bar()
                h.dma(o_x4.rearrange("(cc p) f -> p cc f", p=128)
                      [:, th * 8:(th + 1) * 8, :], x4o.ap())
                h.flush()


DBG_FUSED = False


def build_fused():
    nc = _nc()
    ext_i = lambda n, s, d: nc.dram_tensor(n, s, d, kind="ExternalInput").ap()
    intl = lambda n, s, d: nc.dram_tensor(n, s, d, kind="Internal").ap()

    xg = ext_i("xg", [TOK, C], F32)
    wqkvT0 = ext_i("wqkvT0", [C, 3 * C], F32)
    wprojT0 = ext_i("wprojT0", [C, C], F32)
    bproj0 = ext_i("bproj0", [C], F32)
    wgT = ext_i("wgT", [C, FFN_H], I32)
    wuT = ext_i("wuT", [C, FFN_H], I32)
    wdT = ext_i("wdT", [FFN_H, C], I32)
    wqkvT1 = ext_i("wqkvT1", [C, 3 * C], F32)
    lnw = ext_i("lnw", [C], F32)
    lnb = ext_i("lnb", [C], F32)
    woutT1 = ext_i("woutT1", [C, C], F32)
    cost = ext_i("cost", [NCH_IN * 128, 32], F32)
    ssint = ext_i("ssint", [NCH_IN * 128, 32], F32)
    mown = ext_i("mown", [128, 160], F32)
    mhalo = ext_i("mhalo", [128, 160], F32)
    wq1 = ext_i("wq1", [C, 512], F32)
    wk1 = ext_i("wk1", [C, 512], F32)
    wv1 = ext_i("wv1", [C, 512], F32)
    wo1 = ext_i("wo1", [C, C], F32)
    wff1 = ext_i("wff1", [C, C], F32)
    o_x4 = nc.dram_tensor("x4", [TOK, C], F32, kind="ExternalOutput").ap()

    mk1 = (lambda n, s, d: nc.dram_tensor(n, s, d, kind="ExternalOutput").ap()) \
        if DBG_FUSED else intl
    qf_d = mk1("i_qf", [C, TOK], BF16)
    kvp_in = intl("i_kvp_in", [128, 8 * 65], F32)
    kvp_out = intl("i_kvp_out", [128, 8 * 65], F32)
    x1_d = mk1("i_x1", [TOK, C], BF16)
    x2_d = mk1("i_x2", [TOK, C], BF16)
    halo_in = intl("i_halo_in", [128, C], BF16)
    halo_out = intl("i_halo_out", [256, C], BF16)
    x3_d = intl("i_x3", [TOK, C], BF16)
    x3f_d = intl("i_x3f", [T, C], BF16)
    attn_d2 = intl("i_attn", [512, T], BF16)
    attn_ag = intl("i_attn_ag", [2 * 512, T], BF16)
    sel = ext_i("sel", [128, 2], F32)

    with ExitStack() as gst:
        h = H(nc, gst)
        h.mk_ident()
        h.mk_consts(ALL_CONSTS)

        h.prefix = "A_"
        emit_A(nc, h, xg, wqkvT0, qf_d, None, None, kvp_in, kvp_4d=True)
        h.collective_start("AllReduce", ALU.add, [kvp_in.opt()], [kvp_out.opt()])
        h.prefix = "B_"
        emit_B(nc, h, qf_d, None, None, kvp_out, wprojT0, bproj0, xg, x1_d,
               await_cb=h.collective_await)
        h.prefix = "C_"
        emit_C(nc, h, x1_d, wgT, wuT, wdT, x2_d, halo_in)
        h.collective_start("AllGather", ALU.bypass, [halo_in.opt()], [halo_out.opt()])
        h.prefix = "D_"
        emit_D(nc, h, None, x2_d, halo_out, wqkvT1, lnw, lnb, woutT1, cost,
               ssint, mown, mhalo, x3_d, await_cb=h.collective_await)
        h.collective_start("AllGather", ALU.bypass, [x3_d.opt()], [x3f_d.opt()])
        h.prefix = "E_"
        emit_E(nc, h, x3f_d, wq1, wk1, wv1, attn_d2, a2a_mode=False,
               await_cb=h.collective_await)
        h.collective_start("AllGather", ALU.bypass, [attn_d2.opt()], [attn_ag.opt()])
        h.prefix = "F_"
        emit_F(nc, h, attn_ag, x3_d, wo1, wff1, o_x4, sel=sel,
               await_cb=h.collective_await)
        if DBG_FUSED:
            h.prefix = "Z_"
            d_x3 = nc.dram_tensor("d_x3", [TOK, C], BF16, kind="ExternalOutput").ap()
            d_x3f = nc.dram_tensor("d_x3f", [T, C], BF16, kind="ExternalOutput").ap()
            d_at = nc.dram_tensor("d_at", [2 * 512, T], BF16, kind="ExternalOutput").ap()
            d_kv = nc.dram_tensor("d_kv", [128, 8 * 65], F32, kind="ExternalOutput").ap()
            d_ho = nc.dram_tensor("d_ho", [256, C], BF16, kind="ExternalOutput").ap()
            with h.sbt("dump", [128, 8192], BF16) as dump:
                def dcopy(dst, srcap, rows, cols, dt=BF16):
                    nper = 8192 if dt == BF16 else 4096
                    for r0 in range(0, rows, 128):
                        rr = min(128, rows - r0)
                        for c0 in range(0, cols, nper):
                            ccw = min(nper, cols - c0)
                            sb = dump.ap()[0:rr, 0:ccw] if dt == BF16 else \
                                dump.ap().bitcast(F32)[0:rr, 0:ccw]
                            h.dma(sb, srcap[r0:r0 + rr, c0:c0 + ccw])
                            h.flush()
                            h.dma(dst[r0:r0 + rr, c0:c0 + ccw], sb)
                            h.flush()
                dcopy(d_x3, x3_d, TOK, C)
                dcopy(d_x3f, x3f_d, T, C)
                dcopy(d_at, attn_ag, 1024, T)
                dcopy(d_kv, kvp_out, 128, 520, F32)
                d_kvin = nc.dram_tensor("d_kvin", [128, 8 * 65], F32,
                                        kind="ExternalOutput").ap()
                dcopy(d_kvin, kvp_in, 128, 520, F32)
                dcopy(d_ho, halo_out, 256, C)
    return nc

def _prog(name):
    if name not in _PROG:
        _PROG[name] = {"A": build_A, "B": build_B, "C": build_C,
                       "D": build_D, "E": build_E, "F": build_F,
                       "FUSED": build_fused}[name]()
    return _PROG[name]


def _run(name, in_maps):
    res = run_bass_kernel_spmd(_prog(name), in_maps, core_ids=list(range(8)))
    return res.results


def _rope_tables():
    inv = (10000.0 ** (-(np.arange(0, DH, 2, dtype=np.float64) / DH))).astype(np.float64)
    tpos = np.arange(-128, T, dtype=np.float64)
    fr = tpos[:, None] * inv[None, :]
    fr = np.concatenate([fr, fr], axis=1)
    cos = np.cos(fr[:, 0:32]).astype(np.float32)
    sin = np.sin(fr[:, 0:32]).astype(np.float32)
    return cos, sin


def _kvp_pair(kvp):
    kvp = np.asarray(kvp, np.float32).reshape(64, HEADS, 65)
    out = np.zeros((128, 8, 65), np.float32)
    for hd in range(HEADS):
        par, pr = hd % 2, hd // 2
        out[par * 64:(par + 1) * 64, pr, :] = kvp[:, hd, :]
    return np.ascontiguousarray(out.reshape(128, 8 * 65))


def _band_masks():
    i = np.arange(128)[:, None]
    j = np.arange(160)[None, :]
    m_own = ((j - i >= 0) & (j - i < 32)).astype(np.float32)
    jj = j + 0
    m_halo = (((jj - 128) - i + 128 >= 0) & ((jj - 128) - i + 128 < 32) & (jj >= 128)).astype(np.float32)
    return m_own, m_halo


def kernel(**inputs):
    x = np.ascontiguousarray(np.asarray(inputs["L0_attn_x"], np.float32))
    cos, sin = _rope_tables()
    m_own, m_halo = _band_masks()
    zm = np.zeros_like(m_halo)
    com = {
        "wqkvT0": np.ascontiguousarray(np.asarray(inputs["L0_attn_Wqkv"], np.float32).T),
        "wprojT0": np.ascontiguousarray(np.asarray(inputs["L0_attn_Wproj"], np.float32).T),
        "bproj0": np.ascontiguousarray(np.asarray(inputs["L0_attn_bproj"], np.float32)),
        "wgT": np.ascontiguousarray(np.asarray(inputs["L0_ffn_w_gate"], np.int32).T),
        "wuT": np.ascontiguousarray(np.asarray(inputs["L0_ffn_w_up"], np.int32).T),
        "wdT": np.ascontiguousarray(np.asarray(inputs["L0_ffn_w_down"], np.int32).T),
        "wqkvT1": np.ascontiguousarray(np.asarray(inputs["L1_attn_w_qkv"], np.float32).T),
        "lnw": np.ascontiguousarray(np.asarray(inputs["L1_attn_ln_w"], np.float32)),
        "lnb": np.ascontiguousarray(np.asarray(inputs["L1_attn_ln_b"], np.float32)),
        "woutT1": np.ascontiguousarray(np.asarray(inputs["L1_attn_w_out"], np.float32).T),
        "mown": m_own,
        "wo1": np.ascontiguousarray(np.asarray(inputs["L1_ffn_Wo"], np.float32)),
        "wff1": np.ascontiguousarray(np.asarray(inputs["L1_ffn_Wff"], np.float32)),
    }
    wq1 = np.asarray(inputs["L1_ffn_Wq"], np.float32)
    wk1 = np.asarray(inputs["L1_ffn_Wk"], np.float32)
    wv1 = np.asarray(inputs["L1_ffn_Wv"], np.float32)
    ins = []
    for c in range(8):
        b, r0 = c // 2, (c % 2) * TOK
        hsel = slice((c % 2) * 512, (c % 2) * 512 + 512)
        tsl = slice(r0, r0 + TOK + 128)
        selv = np.zeros((128, 2), np.float32)
        selv[:, c % 2] = 1.0
        m = dict(com)
        m.update({
            "xg": np.ascontiguousarray(x[b, r0:r0 + TOK]),
            "cost": np.ascontiguousarray(cos[tsl]),
            "ssint": np.ascontiguousarray(sin[tsl]),
            "mhalo": (m_halo if r0 > 0 else zm),
            "wq1": np.ascontiguousarray(wq1[:, hsel]),
            "wk1": np.ascontiguousarray(wk1[:, hsel]),
            "wv1": np.ascontiguousarray(wv1[:, hsel]),
            "sel": selv,
        })
        ins.append(m)
    r = _run("FUSED", ins)
    global _LAST_RESULTS
    _LAST_RESULTS = r
    out = np.stack([np.concatenate([r[2 * b]["x4"], r[2 * b + 1]["x4"]], axis=0)
                    for b in range(B)])
    return out.astype(np.float32)


def kernel_multi(L0_attn_x, L0_attn_Wqkv, L0_attn_Wproj, L0_attn_bproj,
                 L0_ffn_w_gate, L0_ffn_w_up, L0_ffn_w_down,
                 L1_attn_ln_w, L1_attn_ln_b, L1_attn_w_qkv, L1_attn_w_out,
                 L1_ffn_Wq, L1_ffn_Wk, L1_ffn_Wv, L1_ffn_Wo, L1_ffn_Wff):
    x = np.ascontiguousarray(np.asarray(L0_attn_x, dtype=np.float32))
    wqkvT0 = np.ascontiguousarray(np.asarray(L0_attn_Wqkv, np.float32).T)
    wprojT0 = np.ascontiguousarray(np.asarray(L0_attn_Wproj, np.float32).T)
    bproj0 = np.ascontiguousarray(np.asarray(L0_attn_bproj, np.float32))
    wgT = np.ascontiguousarray(np.asarray(L0_ffn_w_gate, np.int32).T)
    wuT = np.ascontiguousarray(np.asarray(L0_ffn_w_up, np.int32).T)
    wdT = np.ascontiguousarray(np.asarray(L0_ffn_w_down, np.int32).T)
    lnw = np.ascontiguousarray(np.asarray(L1_attn_ln_w, np.float32))
    lnb = np.ascontiguousarray(np.asarray(L1_attn_ln_b, np.float32))
    wqkvT1 = np.ascontiguousarray(np.asarray(L1_attn_w_qkv, np.float32).T)
    woutT1 = np.ascontiguousarray(np.asarray(L1_attn_w_out, np.float32).T)
    wq1 = np.ascontiguousarray(np.asarray(L1_ffn_Wq, np.float32))
    wk1 = np.ascontiguousarray(np.asarray(L1_ffn_Wk, np.float32))
    wv1 = np.ascontiguousarray(np.asarray(L1_ffn_Wv, np.float32))
    wo1 = np.ascontiguousarray(np.asarray(L1_ffn_Wo, np.float32))
    wff1 = np.ascontiguousarray(np.asarray(L1_ffn_Wff, np.float32))

    def slc(c):
        b, hf = c // 2, c % 2
        return b, hf * TOK

    ins = []
    for c in range(8):
        b, r0 = slc(c)
        ins.append({"xg": np.ascontiguousarray(x[b, r0:r0 + TOK]), "wqkvT": wqkvT0})
    rA = _run("A", ins)

    ins = []
    for c in range(8):
        b, r0 = slc(c)
        sib = c ^ 1
        ins.append({"qf": rA[c]["qf"], "kvp0": _kvp_pair(rA[c]["kvp"]),
                    "kvp1": _kvp_pair(rA[sib]["kvp"]),
                    "wprojT": wprojT0, "bproj": bproj0,
                    "xg": np.ascontiguousarray(x[b, r0:r0 + TOK])})
    rB = _run("B", ins)
    x1 = np.stack([np.concatenate([rB[2 * b]["x1"], rB[2 * b + 1]["x1"]], axis=0)
                   for b in range(B)])

    ins = []
    for c in range(8):
        b, r0 = slc(c)
        ins.append({"x1": np.ascontiguousarray(x1[b, r0:r0 + TOK]),
                    "wgT": wgT, "wuT": wuT, "wdT": wdT})
    rC = _run("C", ins)
    x2 = np.stack([np.concatenate([rC[2 * b]["x2"], rC[2 * b + 1]["x2"]], axis=0)
                   for b in range(B)])

    cos, ssin = _rope_tables()
    m_own, m_halo = _band_masks()
    zeros_mask = np.zeros_like(m_halo)
    ins = []
    for c in range(8):
        b, r0 = slc(c)
        if r0 == 0:
            x2in = np.concatenate([np.zeros((128, C), x2.dtype), x2[b, 0:TOK]], axis=0)
        else:
            x2in = np.ascontiguousarray(x2[b, r0 - 128:r0 + TOK])
        tsl = slice(r0 - 128 + 128, r0 + TOK + 128)
        ins.append({"x2in": x2in, "wqkvT": wqkvT1, "lnw": lnw, "lnb": lnb,
                    "woutT": woutT1,
                    "cost": np.ascontiguousarray(cos[tsl]),
                    "ssint": np.ascontiguousarray(ssin[tsl]),
                    "mown": m_own,
                    "mhalo": (m_halo if r0 > 0 else zeros_mask)})
    rD = _run("D", ins)
    x3 = np.stack([np.concatenate([rD[2 * b]["x3"], rD[2 * b + 1]["x3"]], axis=0)
                   for b in range(B)])

    ins = []
    for c in range(8):
        b, r0 = slc(c)
        hsel = slice((c % 2) * 512, (c % 2) * 512 + 512)
        ins.append({"x3f": np.ascontiguousarray(x3[b]),
                    "wq": np.ascontiguousarray(wq1[:, hsel]),
                    "wk": np.ascontiguousarray(wk1[:, hsel]),
                    "wv": np.ascontiguousarray(wv1[:, hsel])})
    rE = _run("E", ins)
    attn = np.stack([np.concatenate([rE[2 * b]["attn"], rE[2 * b + 1]["attn"]], axis=0)
                     for b in range(B)])

    ins = []
    for c in range(8):
        b, r0 = slc(c)
        ins.append({"attn_own": np.ascontiguousarray(attn[b][:, r0:r0 + TOK]),
                    "x3own": np.ascontiguousarray(x3[b, r0:r0 + TOK]),
                    "wo": wo1, "wff": wff1})
    rF = _run("F", ins)
    out = np.stack([np.concatenate([rF[2 * b]["x4"], rF[2 * b + 1]["x4"]], axis=0)
                    for b in range(B)])
    return out.astype(np.float32)
